# version 01
# baseline (speedup 1.0000x reference)
import numpy as np
import concourse.bass as bass
import concourse.mybir as mybir
from concourse.bass_utils import run_bass_kernel_spmd

F32 = mybir.dt.float32
BF16 = mybir.dt.bfloat16
AF = mybir.ActivationFunctionType
ALU = mybir.AluOpType
AX = mybir.AxisListType

NCORES = 8
SEQ = 4096
D = 1024
NH = 16
NE = 16
TOK = 2 * SEQ
NTILE = TOK // 128
GT = 8
NGRP = NTILE // GT
EPS = 1e-5

SBUF_BASE = 20480
SBUF_TOP = 229344

PP = [(0, 1024), (1024, 2048)]
TRB = [(2048, 2112), (2560, 2624)]
GTR = [(2048, 2176), (2560, 2688)]
PSR = [(3072, 3088), (3584, 3600)]
SC = [(0, 1024), (1024, 2048)]
AVB = [(2048, 2560), (2560, 3072)]
X2P = (3072, 4096)
TRP = [(2048, 2112), (2560, 2624)]


class Emitter:
    ENGS = ("sync", "gpsimd", "tensor", "vector", "scalar")

    def __init__(self, nc):
        self.nc = nc
        self.prog = {e: [] for e in self.ENGS}
        self.sems = {}
        self.cnt = {}
        self.hw = {e: {} for e in self.ENGS}
        self.sboff = SBUF_BASE
        self.bank = {b: {"w": None, "r": []} for b in range(8)}

    def sb(self, name, shape, dtype):
        per = int(np.prod(shape[1:])) * mybir.dt.size(dtype)
        off = (self.sboff + 31) // 32 * 32
        self.sboff = off + per
        return self.nc.alloc_sbuf_tensor_at(name, shape, dtype, offset=off).ap()

    def sb_at(self, name, shape, dtype, off):
        return self.nc.alloc_sbuf_tensor_at(name, shape, dtype, offset=off).ap()

    def sem(self, name):
        h = self.nc.alloc_semaphore(name=name)
        self.sems[name] = h
        self.cnt[name] = 0
        return h

    def inc(self, name, by=1):
        self.cnt[name] += by
        return (name, self.cnt[name])

    def evt(self, name):
        return (name, self.cnt[name])

    def emit(self, eng, fn):
        self.prog[eng].append(fn)

    def wait(self, eng, evt):
        if evt is None:
            return
        sname, val = evt
        if val <= 0 or self.hw[eng].get(sname, 0) >= val:
            return
        self.hw[eng][sname] = val
        sem = self.sems[sname]
        self.emit(eng, lambda E, s=sem, v=val: E.wait_ge(s, v))

    def _banks(self, c0, c1):
        return range(c0 // 512, (c1 - 1) // 512 + 1)

    def psum_write(self, eng, c0, c1):
        for b in self._banks(c0, c1):
            st = self.bank[b]
            for (s, c, e) in st["r"]:
                if e != eng:
                    self.wait(eng, (s, c))
            if st["w"] is not None and st["w"][2] != eng:
                self.wait(eng, (st["w"][0], st["w"][1]))

    def psum_write_done(self, eng, c0, c1, evt):
        for b in self._banks(c0, c1):
            self.bank[b] = {"w": (evt[0], evt[1], eng), "r": []}

    def psum_read(self, eng, c0, c1):
        for b in self._banks(c0, c1):
            st = self.bank[b]
            if st["w"] is not None and st["w"][2] != eng:
                self.wait(eng, (st["w"][0], st["w"][1]))

    def psum_read_done(self, eng, c0, c1, evt):
        for b in self._banks(c0, c1):
            self.bank[b]["r"].append((evt[0], evt[1], eng))


def build_program(phases=("pre", "p1", "p2", "p3", "p4"), dbg=(), paranoid=False, triv_ln=False):
    nc = bass.Bass("TRN2", target_bir_lowering=False)
    em = Emitter(nc)

    xin = nc.dram_tensor("xin", [NTILE, 128, D], F32, kind="ExternalInput").ap()
    wqt = nc.dram_tensor("wqt", [D, D], F32, kind="ExternalInput").ap()
    wkt = nc.dram_tensor("wkt", [D, D], F32, kind="ExternalInput").ap()
    wvt = nc.dram_tensor("wvt", [D, D], F32, kind="ExternalInput").ap()
    wot = nc.dram_tensor("wot", [D, D], F32, kind="ExternalInput").ap()
    bo_in = nc.dram_tensor("bo", [1, D], F32, kind="ExternalInput").ap()
    ct_in = nc.dram_tensor("ct", [D, NE], F32, kind="ExternalInput").ap()
    eb_in = nc.dram_tensor("eb", [1, NE], F32, kind="ExternalInput").ap()
    wrt_in = nc.dram_tensor("wrt", [NE, D, D], F32, kind="ExternalInput").ap()
    br_in = nc.dram_tensor("br", [NE, D], F32, kind="ExternalInput").ap()
    wst_in = nc.dram_tensor("wst", [2, D, D], F32, kind="ExternalInput").ap()
    bs_in = nc.dram_tensor("bs", [2, D], F32, kind="ExternalInput").ap()
    out = nc.dram_tensor("out", [NTILE, 128, D], F32, kind="ExternalOutput").ap()

    wr_bf = nc.dram_tensor("wr_bf", [NE, 128, 8, D], BF16, kind="Internal").ap()
    wsh_s = nc.dram_tensor("wsh_s", [128, 8, D], BF16, kind="Internal").ap()
    wq_s = nc.dram_tensor("wq_s", [128, 8, D], BF16, kind="Internal").ap()
    wk_s = nc.dram_tensor("wk_s", [128, 8, D], BF16, kind="Internal").ap()
    wv_s = nc.dram_tensor("wv_s", [128, 8, D], BF16, kind="Internal").ap()
    wo_s = nc.dram_tensor("wo_s", [128, 8, D], BF16, kind="Internal").ap()
    x1_s = nc.dram_tensor("x1_s", [NTILE, 128, D], F32, kind="Internal").ap()
    x2_s = nc.dram_tensor("x2_s", [NTILE, 128, D], F32, kind="Internal").ap()
    o_s = nc.dram_tensor("o_s", [2, SEQ, D], BF16, kind="Internal").ap()

    dbg_t = {}
    for name in dbg:
        dbg_t[name] = nc.dram_tensor("dbg_" + name, [NTILE, 128, D], F32,
                                     kind="ExternalOutput").ap()

    ctop = SBUF_TOP

    def const_sb(name, shape, dtype):
        nonlocal ctop
        per = int(np.prod(shape[1:])) * mybir.dt.size(dtype)
        ctop = (ctop - per) // 32 * 32
        return em.sb_at(name, shape, dtype, ctop)

    ident_bf = const_sb("ident_bf", [128, 128], BF16)
    ident_f = const_sb("ident_f", [128, 128], F32)
    ones_bf = const_sb("ones_bf", [1, 128], BF16)
    ones_f = const_sb("ones_f", [1, 128], F32)
    ct_bf = const_sb("ct_bf", [128, 8, NE], BF16)
    brh_bf = const_sb("brh_bf", [NE, D], BF16)
    bsh4_bf = const_sb("bsh4_bf", [1, D], BF16)
    eb_bc = const_sb("eb_bc", [128, NE], F32)
    epsv = const_sb("epsv", [128, 1], F32)
    bo_bc = const_sb("bo_bc", [128, D], F32)
    CONST_LIMIT = ctop

    psa = nc.alloc_psum_tensor("psarena", [128, 4096], F32).ap()

    def psv(c0, c1, bf16=False):
        v = psa[:, c0:c1]
        if bf16:
            v = v.bitcast(BF16)
        return v

    DMA_SEM_NAMES = ("sQ", "sGd", "sCast", "sPre", "sWsh", "sWL2", "sWL3",
                     "sx0", "sx1", "ss0", "ss1", "wb0", "wb1",
                     "so00", "so01", "so10", "so11", "oq0", "oq1", "sxs0", "sxs1", "sx1t0", "sx1t1",
                     "sDbg")
    for s in ("sP", "sV", "sA", "sGc") + DMA_SEM_NAMES:
        em.sem(s)

    def barrier_all(eng):
        if not paranoid:
            return
        for sn in ("sP", "sV", "sA", "sGc", "sGd", "sQ"):
            em.wait(eng, em.evt(sn))

    def vop(make):
        barrier_all("vector")
        evt = em.inc("sV")
        prev = evt[1] - 1
        sem = em.sems["sV"]
        def fn(V, m=make, s=sem, p=prev):
            if p > 0:
                V.wait_ge(s, p)
            m(V).then_inc(s, 1)
        em.emit("vector", fn)
        return evt

    def aop(make):
        barrier_all("scalar")
        evt = em.inc("sA")
        prev = evt[1] - 1
        sem = em.sems["sA"]
        def fn(S, m=make, s=sem, p=prev):
            if p > 0:
                S.wait_ge(s, p)
            m(S).then_inc(s, 1)
        em.emit("scalar", fn)
        return evt

    def gop(make):
        barrier_all("gpsimd")
        evt = em.inc("sGc")
        prev = evt[1] - 1
        sem = em.sems["sGc"]
        def fn(G, m=make, s=sem, p=prev):
            if p > 0:
                G.wait_ge(s, p)
            m(G).then_inc(s, 1)
        em.emit("gpsimd", fn)
        return evt

    def pop(make, milestone=True):
        barrier_all("tensor")
        if milestone:
            evt = em.inc("sP")
            sem = em.sems["sP"]
            em.emit("tensor", lambda T, m=make, s=sem: m(T).then_inc(s, 1))
            return evt
        if paranoid:
            evt = em.inc("sP")
            sem = em.sems["sP"]
            em.emit("tensor", lambda T, m=make, s2=sem: m(T).then_inc(s2, 1))
            return evt
        em.emit("tensor", lambda T, m=make: m(T))
        return None

    def dma(queue, outap, inap, sem_name):
        barrier_all(queue)
        evt = em.inc(sem_name, 16)
        sem = em.sems[sem_name]
        em.emit(queue, lambda E, o=outap, i=inap, s=sem:
                E.dma_start(out=o, in_=i).then_inc(s, 16))
        return evt

    pre_off = em.sboff
    pw0 = em.sb("pw0", [128, 8, D], F32)
    pw1 = em.sb("pw1", [128, 8, D], F32)
    pst = em.sb("pst", [NE, D], F32)
    pbs = em.sb("pbs", [1, 2 * D], F32)
    pbo = em.sb("pbo", [1, D], F32)
    peb = em.sb("peb", [1, NE], F32)
    psh_bf = em.sb("psh_bf", [128, 8, D], BF16)

    cast_evts = {e: None for e in range(NE)}
    cast_w = {}
    e_brh = e_bsh = None
    st_sh = None
    pre_v_done = None
    pre_p_done = None
    pre_q_done = None

    if "pre" in phases:
        gop(lambda G: G.memset(ident_bf, 0.0))
        gop(lambda G: G.affine_select(out=ident_bf, in_=ident_bf,
                                      compare_op=ALU.not_equal, fill=1.0, base=0,
                                      pattern=[[-1, 128]], channel_multiplier=1))
        gop(lambda G: G.memset(ident_f, 0.0))
        gop(lambda G: G.affine_select(out=ident_f, in_=ident_f,
                                      compare_op=ALU.not_equal, fill=1.0, base=0,
                                      pattern=[[-1, 128]], channel_multiplier=1))
        gop(lambda G: G.memset(ones_bf, 1.0))
        gop(lambda G: G.memset(ones_f, 1.0))
        gop(lambda G: G.memset(epsv, EPS))
        e_gc = em.evt("sGc")
        em.wait("tensor", e_gc)
        em.wait("scalar", e_gc)
        em.wait("vector", e_gc)
        cast_w["ct"] = dma("gpsimd", ct_bf, ct_in.rearrange("(c p) e -> p c e", p=128), "sCast")
        for e in range(NE):
            cast_evts[e] = dma("gpsimd", wr_bf[e],
                               wrt_in[e].rearrange("(c p) o -> p c o", p=128), "sCast")
        cast_w["q"] = dma("gpsimd", wq_s, wqt.rearrange("(c p) o -> p c o", p=128), "sCast")
        cast_w["k"] = dma("gpsimd", wk_s, wkt.rearrange("(c p) o -> p c o", p=128), "sCast")
        cast_w["v"] = dma("gpsimd", wv_s, wvt.rearrange("(c p) o -> p c o", p=128), "sCast")
        cast_w["o"] = dma("gpsimd", wo_s, wot.rearrange("(c p) o -> p c o", p=128), "sCast")
        cast_all = em.evt("sCast")
        cast_w = {k: cast_all for k in cast_w}
        cast_evts = {e: cast_all for e in cast_evts}

        ld_w0 = dma("sync", pw0, wst_in[0].rearrange("(c p) o -> p c o", p=128), "sPre")
        ld_w1 = dma("sync", pw1, wst_in[1].rearrange("(c p) o -> p c o", p=128), "sPre")
        ld_br = dma("sync", pst, br_in, "sPre")
        ld_bs = dma("sync", pbs, bs_in.rearrange("a d -> (a d)").unsqueeze(0), "sPre")
        ld_bo = dma("sync", pbo, bo_in, "sPre")
        ld_eb = dma("sync", peb, eb_in, "sPre")

        pre_batch = em.evt("sPre")
        em.wait("vector", pre_batch)
        vop(lambda V: V.tensor_tensor(pw0.rearrange("p c o -> p (c o)"),
                                      pw0.rearrange("p c o -> p (c o)"),
                                      pw1.rearrange("p c o -> p (c o)"), op=ALU.add))
        e_sh = vop(lambda V: V.tensor_scalar(psh_bf.rearrange("p c o -> p (c o)"),
                                             pw0.rearrange("p c o -> p (c o)"),
                                             0.5, None, op0=ALU.mult))
        em.wait("sync", e_sh)
        st_sh = dma("sync", wsh_s, psh_bf, "sWsh")

        e_brh = vop(lambda V: V.tensor_scalar(brh_bf, pst, 0.5, None, op0=ALU.mult))
        vop(lambda V: V.tensor_tensor(pbs[0:1, 0:D], pbs[0:1, 0:D], pbs[0:1, D:2 * D], op=ALU.add))
        e_bsh = vop(lambda V: V.tensor_scalar(bsh4_bf, pbs[0:1, 0:D], 0.25, None, op0=ALU.mult))

        em.wait("tensor", pre_batch)
        em.psum_write("tensor", 2048, 2064)
        e_mm = pop(lambda T: T.matmul(psv(2048, 2064), ones_f, peb, start=True, stop=True))
        em.psum_write_done("tensor", 2048, 2064, e_mm)
        em.psum_read("vector", 2048, 2064)
        e_ebc = vop(lambda V: V.tensor_copy(eb_bc, psv(2048, 2064)))
        em.psum_read_done("vector", 2048, 2064, e_ebc)

        em.wait("tensor", pre_batch)
        em.psum_write("tensor", 0, 1024)
        pop(lambda T: T.matmul(psv(0, 512), ones_f, pbo[:, 0:512], start=True, stop=True),
            milestone=False)
        e_mm2 = pop(lambda T: T.matmul(psv(512, 1024), ones_f, pbo[:, 512:1024],
                                       start=True, stop=True))
        em.psum_write_done("tensor", 0, 1024, e_mm2)
        em.psum_read("vector", 0, 1024)
        e_boc = vop(lambda V: V.tensor_copy(bo_bc, psv(0, 1024)))
        em.psum_read_done("vector", 0, 1024, e_boc)

        pre_v_done = em.evt("sV")
        pre_p_done = em.evt("sP")

    def alloc_ffn_area(px):
        em.sboff = pre_off
        t = {}
        t["u_fm"] = em.sb(px + "ufm", [128, 2, 8, GT * 128], BF16)
        t["xg"] = em.sb(px + "xg", [128, 2, GT, D], F32)
        t["acc"] = em.sb(px + "acc", [128, GT, D], F32)
        t["wbuf"] = em.sb(px + "wbuf", [128, 2, 8, D], BF16)
        t["uh"] = em.sb(px + "uh", [128, 2, D], BF16)
        t["xm"] = em.sb(px + "xm", [128, 2, D], F32)
        t["sq"] = em.sb(px + "sq", [128, D], F32)
        t["ut"] = em.sb(px + "ut", [128, D], F32)
        t["lnst"] = em.sb(px + "lnst", [128, 2, GT, 2], F32)
        t["g"] = em.sb(px + "g", [128, 2, GT, NE], F32)
        t["gt"] = em.sb(px + "gt", [16, 2, GT, 128], BF16)
        t["rsc"] = em.sb(px + "rsc", [128, 128], F32)
        assert em.sboff <= CONST_LIMIT, f"FFN area overflow {em.sboff}"
        return t

    def emit_ln(t, src_ap, uh_slot, halved, store_ln, ln_dep):
        xm = t["xm"][:, uh_slot]
        sq = t["sq"]
        rs = t["rsc"]
        msum = rs[:, 0:1]
        if store_ln is not None:
            mneg = t["lnst"][:, store_ln[0], store_ln[1], 0:1]
            rstd = t["lnst"][:, store_ln[0], store_ln[1], 1:2]
        else:
            mneg = rs[:, 1:2]
            rstd = rs[:, 4:5]
        vsum = rs[:, 2:3]
        stdv = rs[:, 3:4]
        if ln_dep is not None:
            em.wait("vector", ln_dep)
        if triv_ln == 1:
            vop(lambda V: V.memset(mneg, 0.0))
            vop(lambda V: V.memset(rstd, 1.0))
            s2t = 0.5 if halved else 1.0
            e_uh = vop(lambda V, s3=s2t: V.tensor_scalar(
                t["uh"][:, uh_slot], src_ap, s3, None, op0=ALU.mult))
            return e_uh
        if triv_ln == 2:
            vop(lambda V: V.tensor_reduce(msum, src_ap, axis=AX.X, op=ALU.add))
            vop(lambda V: V.tensor_scalar(mneg, msum, -1.0 / D, None, op0=ALU.mult))
            vop(lambda V: V.memset(rstd, 1.0))
            s2t = 0.5 if halved else 1.0
            e_uh = vop(lambda V, s3=s2t: V.tensor_scalar(
                t["uh"][:, uh_slot], src_ap, mneg, s3, op0=ALU.add, op1=ALU.mult))
            return e_uh
        vop(lambda V: V.tensor_reduce(msum, src_ap, axis=AX.X, op=ALU.add))
        vop(lambda V: V.tensor_scalar(mneg, msum, -1.0 / D, None, op0=ALU.mult))
        e_xm = vop(lambda V: V.tensor_scalar(xm, src_ap, mneg, None, op0=ALU.add))
        em.wait("scalar", e_xm)
        e_sq = aop(lambda S: S.activation(sq, xm, AF.Square))
        em.wait("vector", e_sq)
        e_vs = vop(lambda V: V.tensor_reduce(vsum, sq, axis=AX.X, op=ALU.add))
        em.wait("scalar", e_vs)
        e_st = aop(lambda S: S.activation(stdv, vsum, AF.Sqrt, scale=1.0 / D, bias=epsv))
        em.wait("vector", e_st)
        vop(lambda V: V.reciprocal(rstd, stdv))
        s2 = 0.5 if halved else 1.0
        e_uh = vop(lambda V, s=s2: V.tensor_scalar(
            t["uh"][:, uh_slot], xm, rstd, s, op0=ALU.mult, op1=ALU.mult))
        return e_uh

    def emit_transposes(t, uh_slot, dst_fn, n_tr):
        last = None
        for c in range(8):
            slot = n_tr[0] % 2
            n_tr[0] += 1
            c0, c1 = TRB[slot]
            em.psum_write("tensor", c0, c1)
            trv = psv(c0, c1, bf16=True)
            e_tr = pop(lambda T, c2=c, tv=trv, s=uh_slot: T.transpose(
                tv, t["uh"][:, s][:, c2 * 128:(c2 + 1) * 128], ident_bf))
            em.psum_write_done("tensor", c0, c1, e_tr)
            em.psum_read("vector", c0, c1)
            e_ev = vop(lambda V, d=dst_fn(c), tv=trv: V.tensor_copy(d, tv))
            em.psum_read_done("vector", c0, c1, e_ev)
            last = e_ev
        return last

    def emit_router(t, parity, i, e_ufm, n_r):
        slot = n_r[0] % 2
        n_r[0] += 1
        c0, c1 = PSR[slot]
        prv = psv(c0, c1)
        em.wait("tensor", e_ufm)
        em.wait("tensor", cast_w.get("ct"))
        em.psum_write("tensor", c0, c1)
        for c in range(8):
            e_mm = pop(lambda T, c2=c: T.matmul(
                prv, t["u_fm"][:, parity, c2, i * 128:(i + 1) * 128],
                ct_bf[:, c2, :], start=(c2 == 0), stop=(c2 == 7)),
                milestone=(c == 7))
        em.psum_write_done("tensor", c0, c1, e_mm)
        rs = t["rsc"]
        scf, e_r, s_r = rs[:, 16:32], rs[:, 32:48], rs[:, 48:64]
        msk, mskd = rs[:, 64:80], rs[:, 80:96]
        mn, den, rden = rs[:, 96:97], rs[:, 97:98], rs[:, 98:99]
        m1, m2 = rs[:, 99:100], rs[:, 100:101]
        em.psum_read("vector", c0, c1)
        e_sc = vop(lambda V: V.scalar_tensor_tensor(scf, prv, 2.0, eb_bc,
                                                    op0=ALU.mult, op1=ALU.add))
        em.psum_read_done("vector", c0, c1, e_sc)
        e_mn = vop(lambda V: V.tensor_reduce(mn, scf, axis=AX.X, op=ALU.max, negate=True))
        em.wait("scalar", e_mn)
        e_exp = aop(lambda S: S.activation(e_r, scf, AF.Exp, bias=mn, scale=1.0,
                                           accum_out=den))
        em.wait("vector", e_exp)
        vop(lambda V: V.reciprocal(rden, den))
        vop(lambda V: V.tensor_scalar(s_r, e_r, rden, None, op0=ALU.mult))
        vop(lambda V: V.tensor_reduce(m1, s_r, axis=AX.X, op=ALU.max))
        vop(lambda V: V.tensor_scalar(msk, s_r, m1, None, op0=ALU.is_ge))
        vop(lambda V: V.scalar_tensor_tensor(mskd, msk, -1e9, s_r, op0=ALU.mult, op1=ALU.add))
        vop(lambda V: V.tensor_reduce(m2, mskd, axis=AX.X, op=ALU.max))
        vop(lambda V: V.tensor_scalar(msk, s_r, m2, None, op0=ALU.is_ge))
        e_g = vop(lambda V: V.tensor_tensor(t["g"][:, parity, i, :], s_r, msk, op=ALU.mult))
        return e_g

    def emit_gtr(t, parity, i, e_g, n_tr):
        em.wait("tensor", e_g)
        slot = n_tr[0] % 2
        n_tr[0] += 1
        c0, c1 = GTR[slot]
        gv = psv(c0, c1)[0:16, :]
        em.psum_write("tensor", c0, c1)
        e_tr = pop(lambda T: T.transpose(gv, t["g"][:, parity, i, :], ident_f))
        em.psum_write_done("tensor", c0, c1, e_tr)
        em.psum_read("vector", c0, c1)
        e_ev = vop(lambda V: V.tensor_copy(t["gt"][:, parity, i, :], gv))
        em.psum_read_done("vector", c0, c1, e_ev)
        return e_ev

    def emit_ffn_call(t, src_dram, dst_dram, first_call, src_ready, dbg_name):
        n_tr, n_r, n_pp = [0], [0], [0]
        ufm_ready = [None, None]
        g_ready = [[None] * GT, [None] * GT]
        gt_ready = [[None] * GT, [None] * GT]
        uh_free = [None, None]
        wb_free = [None, None]
        xg_free = [None, None]
        acc_done = [None] * GT
        spills = [None] * NTILE
        spill_batch = [None, None]

        def stage_a(g):
            parity = g % 2
            sx = "sx0" if parity == 0 else "sx1"
            em.wait("sync", xg_free[parity])
            for i in range(GT):
                ti = g * GT + i
                dma("sync", t["xg"][:, parity, i, :], src_dram[ti], sx)
            ldx_batch = em.evt(sx)
            last_ufm = None
            for i in range(GT):
                em.wait("vector", ldx_batch)
                e_uh = emit_ln(t, t["xg"][:, parity, i, :], i % 2, True,
                               (parity, i), uh_free[i % 2])
                em.wait("tensor", e_uh)
                dst_fn = lambda c, i2=i, p2=parity: t["u_fm"][:, p2, c, i2 * 128:(i2 + 1) * 128]
                e_ev = emit_transposes(t, i % 2, dst_fn, n_tr)
                uh_free[i % 2] = e_ev
                last_ufm = e_ev
                if triv_ln == 3:
                    vop(lambda V: V.memset(t["g"][:, parity, i, :], 0.0))
                    e_g = vop(lambda V: V.memset(t["g"][:, parity, i, 0:2], 0.5))
                else:
                    e_g = emit_router(t, parity, i, e_ev, n_r)
                g_ready[parity][i] = e_g
                gt_ready[parity][i] = emit_gtr(t, parity, i, e_g, n_tr)
            ufm_ready[parity] = last_ufm

        def stage_b(g):
            parity = g % 2
            acc_init = [None] * GT
            for i in range(GT):
                if g > 0:
                    em.wait("vector", spill_batch[(g - 1) % 2])
                mneg = t["lnst"][:, parity, i, 0:1]
                rstd = t["lnst"][:, parity, i, 1:2]
                xgv = t["xg"][:, parity, i, :]
                vop(lambda V, m=mneg, r=rstd, x=xgv: V.tensor_scalar(
                    t["ut"], x, m, r, op0=ALU.add, op1=ALU.mult))
                acc_init[i] = vop(lambda V, x=xgv, i2=i: V.scalar_tensor_tensor(
                    t["acc"][:, i2, :], t["ut"], 0.5, x, op0=ALU.mult, op1=ALU.add))
            w_evts = []

            def emit_stream(e):
                src = wr_bf[e] if e < 16 else wsh_s
                slot = e % 2
                em.wait("gpsimd", wb_free[slot])
                if first_call:
                    em.wait("gpsimd", cast_evts[e] if e < 16 else st_sh)
                w_evts.append(dma("gpsimd", t["wbuf"][:, slot], src,
                                  "wb0" if slot == 0 else "wb1"))

            emit_stream(0)
            emit_stream(1)
            for e in range(17):
                slot = e % 2
                em.wait("tensor", w_evts[e])
                em.wait("tensor", ufm_ready[parity])
                for i in range(GT):
                    pslot = n_pp[0] % 2
                    n_pp[0] += 1
                    c0, c1 = PP[pslot]
                    ppv = psv(c0, c1)
                    em.psum_write("tensor", c0, c1)
                    ev = None
                    if e == 16:
                        em.wait("tensor", gt_ready[parity][i])
                        em.wait("tensor", e_brh)
                        em.wait("tensor", e_bsh)
                        for h in range(2):
                            pop(lambda T, i2=i, h2=h, pv=ppv: T.matmul(
                                pv[:, h2 * 512:(h2 + 1) * 512],
                                t["gt"][:, parity, i2, :],
                                brh_bf[:, h2 * 512:(h2 + 1) * 512],
                                start=True, stop=False), milestone=False)
                            pop(lambda T, h2=h, pv=ppv: T.matmul(
                                pv[:, h2 * 512:(h2 + 1) * 512], ones_bf,
                                bsh4_bf[:, h2 * 512:(h2 + 1) * 512],
                                start=False, stop=False), milestone=False)
                        for c in range(8):
                            for h in range(2):
                                ml = (c == 7 and h == 1)
                                ev2 = pop(lambda T, i2=i, c2=c, h2=h, pv=ppv, sl=slot: T.matmul(
                                    pv[:, h2 * 512:(h2 + 1) * 512],
                                    t["u_fm"][:, parity, c2, i2 * 128:(i2 + 1) * 128],
                                    t["wbuf"][:, sl, c2, h2 * 512:(h2 + 1) * 512],
                                    start=False, stop=(c2 == 7)), milestone=ml)
                                if ml:
                                    ev = ev2
                    else:
                        for c in range(8):
                            for h in range(2):
                                ml = (c == 7 and h == 1)
                                ev2 = pop(lambda T, i2=i, c2=c, h2=h, pv=ppv, sl=slot: T.matmul(
                                    pv[:, h2 * 512:(h2 + 1) * 512],
                                    t["u_fm"][:, parity, c2, i2 * 128:(i2 + 1) * 128],
                                    t["wbuf"][:, sl, c2, h2 * 512:(h2 + 1) * 512],
                                    start=(c2 == 0), stop=(c2 == 7)), milestone=ml)
                                if ml:
                                    ev = ev2
                    em.psum_write_done("tensor", c0, c1, ev)
                    em.wait("vector", ev)
                    if e == 0:
                        em.wait("vector", acc_init[i])
                    em.psum_read("vector", c0, c1)
                    if e == 16:
                        e_evac = vop(lambda V, i2=i, pv=ppv: V.tensor_tensor(
                            t["acc"][:, i2, :], t["acc"][:, i2, :], pv, op=ALU.add))
                    else:
                        em.wait("vector", g_ready[parity][i])
                        gcol = t["g"][:, parity, i, e:e + 1]
                        e_evac = vop(lambda V, i2=i, gc=gcol, pv=ppv: V.scalar_tensor_tensor(
                            t["acc"][:, i2, :], pv, gc, t["acc"][:, i2, :],
                            op0=ALU.mult, op1=ALU.add))
                    em.psum_read_done("vector", c0, c1, e_evac)
                    if e == 16:
                        acc_done[i] = e_evac
                wb_free[slot] = em.evt("sP")
                if e + 2 <= 16:
                    emit_stream(e + 2)
            ssem = "ss0" if parity == 0 else "ss1"
            for i in range(GT):
                ti = g * GT + i
                em.wait("sync", acc_done[i])
                spills[ti] = dma("sync", dst_dram[ti], t["acc"][:, i, :], ssem)
                if dbg_name is not None:
                    spills[ti] = dma("sync", dbg_t[dbg_name][ti], t["acc"][:, i, :], ssem)
            spill_batch[parity] = em.evt(ssem)
            xg_free[parity] = acc_done[GT - 1]

        if first_call and "pre" in phases:
            em.wait("sync", pre_v_done)
            em.wait("sync", pre_p_done)
            em.wait("sync", st_sh)
            em.wait("vector", st_sh)
        stage_a(0)
        for g in range(NGRP):
            stage_b(g)
            if g + 1 < NGRP:
                stage_a(g + 1)
        return spills

    NROT_DVE = 11

    def alloc_mix_area():
        em.sboff = pre_off
        t = {}
        t["wq"] = em.sb("m_wq", [128, 8, D], BF16)
        t["wk"] = em.sb("m_wk", [128, 8, D], BF16)
        t["wv"] = em.sb("m_wv", [128, 8, D], BF16)
        t["x1t"] = em.sb("m_x1t", [128, 2, D], F32)
        t["uh"] = em.sb("m_uh", [128, 2, D], BF16)
        t["xm"] = em.sb("m_xm", [128, 2, D], F32)
        t["sq"] = em.sb("m_sq", [128, D], F32)
        t["rsc"] = em.sb("m_rsc", [128, 16], F32)
        t["yfm"] = em.sb("m_yfm", [128, 2, 8, 128], BF16)
        t["qc"] = em.sb("m_qc", [128, 2, 2048], BF16)
        t["kc"] = em.sb("m_kc", [128, 2, 2048], BF16)
        t["vc"] = em.sb("m_vc", [128, 2, 2048], BF16)
        t["ar"] = em.sb("m_ar", [128, 2, 256], F32)
        t["as"] = em.sb("m_as", [128, 256], F32)
        t["es"] = em.sb("m_es", [128, 2, 256], F32)
        t["pr"] = em.sb("m_pr", [128, 2, 256], BF16)
        t["mx"] = em.sb("m_mx", [128, 16], F32)
        t["den"] = em.sb("m_den", [128, 16], F32)
        t["rd"] = em.sb("m_rd", [128, 2, 16], F32)
        t["pd"] = em.sb("m_pd", [128, 1024], BF16)
        t["pg"] = em.sb("m_pg", [128, 1024], BF16)
        t["oa"] = em.sb("m_oa", [128, 2, D], F32)
        t["ob"] = em.sb("m_ob", [128, 2, D], BF16)
        assert em.sboff <= CONST_LIMIT, f"mix overflow {em.sboff}"
        return t

    MQ, MK, MV = (0, 1024), (1024, 2048), (2048, 3072)
    MTR = [(3072, 3136), (3584, 3648)]

    def emit_mix(t, x1_spills):
        if "pre" in phases:
            em.wait("sync", pre_v_done)
            em.wait("sync", pre_p_done)
            em.wait("sync", st_sh)
            em.wait("vector", st_sh)
            em.wait("scalar", st_sh)
            em.wait("sync", cast_w.get("v"))
        dma("sync", t["wq"], wq_s, "sWL2")
        dma("sync", t["wk"], wk_s, "sWL2")
        dma("sync", t["wv"], wv_s, "sWL2")
        em.wait("tensor", em.evt("sWL2"))
        n_tr = [0]
        uh_free = [None, None]
        x1_free = [None, None]
        buf_free = [None, None]
        buf_free_g = [None, None]
        ob_free = [None, None]
        ospill = [None] * NTILE
        for ti in range(NTILE):
            sl = ti % 2
            seq = ti // 32
            em.wait("sync", x1_free[sl])
            ldx = dma("sync", t["x1t"][:, sl], x1_s[ti], "sx1t0" if sl == 0 else "sx1t1")
            em.wait("vector", ldx)
            e_uh = emit_ln(t, t["x1t"][:, sl], sl, False, None, uh_free[sl])
            em.wait("tensor", e_uh)
            dst_fn = lambda c, s2=sl: t["yfm"][:, s2, c, :]
            e_yfm = emit_transposes(t, sl, dst_fn, n_tr)
            uh_free[sl] = e_yfm
            em.wait("tensor", e_yfm)
            em.psum_write("tensor", MQ[0], MQ[1])
            em.psum_write("tensor", MK[0], MK[1])
            em.psum_write("tensor", MV[0], MV[1])
            e_q = e_k = e_v = None
            for c in range(8):
                lhs = t["yfm"][:, sl, c, :]
                for (w, rng) in (("wq", MQ), ("wk", MK), ("wv", MV)):
                    for h in range(2):
                        ml = (c == 7)
                        ev = pop(lambda T, l=lhs, w2=w, c2=c, h2=h, r=rng: T.matmul(
                            psv(r[0] + h2 * 512, r[0] + (h2 + 1) * 512), l,
                            t[w2][:, c2, h2 * 512:(h2 + 1) * 512],
                            start=(c2 == 0), stop=(c2 == 7)),
                            milestone=(ml and h == 1))
                        if ml and h == 1:
                            if w == "wq":
                                e_q = ev
                            elif w == "wk":
                                e_k = ev
                            else:
                                e_v = ev
            em.psum_write_done("tensor", MQ[0], MQ[1], e_q)
            em.psum_write_done("tensor", MK[0], MK[1], e_k)
            em.psum_write_done("tensor", MV[0], MV[1], e_v)
            em.wait("vector", buf_free[sl])
            em.wait("vector", e_q)
            em.psum_read("vector", MQ[0], MQ[1])
            e1 = vop(lambda V, s2=sl: V.tensor_copy(t["qc"][:, s2, 0:1024], psv(*MQ)))
            em.psum_read_done("vector", MQ[0], MQ[1], e1)
            vop(lambda V, s2=sl: V.tensor_copy(t["qc"][:, s2, 1024:2048],
                                               t["qc"][:, s2, 0:1024]))
            em.wait("vector", e_k)
            em.psum_read("vector", MK[0], MK[1])
            e2 = vop(lambda V, s2=sl: V.tensor_copy(t["kc"][:, s2, 0:1024], psv(*MK)))
            em.psum_read_done("vector", MK[0], MK[1], e2)
            vop(lambda V, s2=sl: V.tensor_copy(t["kc"][:, s2, 1024:2048],
                                               t["kc"][:, s2, 0:1024]))
            em.wait("vector", e_v)
            em.psum_read("vector", MV[0], MV[1])
            e3 = vop(lambda V, s2=sl: V.tensor_copy(t["vc"][:, s2, 0:1024], psv(*MV)))
            em.psum_read_done("vector", MV[0], MV[1], e3)
            e_kv = vop(lambda V, s2=sl: V.tensor_copy(t["vc"][:, s2, 1024:2048],
                                                      t["vc"][:, s2, 0:1024]))
            em.wait("gpsimd", e_kv)
            em.wait("gpsimd", buf_free_g[sl])
            att_evts = []
            pg_rd = [None]
            for r in range(16):
                arow = t["ar"][:, sl, 16 * r:16 * (r + 1)]
                ar3 = arow.rearrange("p (i o) -> p i o", o=1)
                if r < NROT_DVE:
                    vop(lambda V, r2=r, s2=sl: V.tensor_tensor(
                        t["pd"], t["qc"][:, s2, 0:1024],
                        t["kc"][:, s2, 64 * r2:64 * r2 + 1024], op=ALU.mult))
                    ev = vop(lambda V, a3=ar3: V.tensor_reduce(
                        a3, t["pd"].rearrange("p (i d) -> p i d", d=64),
                        axis=AX.X, op=ALU.add))
                else:
                    em.wait("gpsimd", pg_rd[0])
                    e_m = gop(lambda G, r2=r, s2=sl: G.tensor_tensor(
                        t["pg"], t["qc"][:, s2, 0:1024],
                        t["kc"][:, s2, 64 * r2:64 * r2 + 1024], op=ALU.mult))
                    em.wait("vector", e_m)
                    ev = vop(lambda V, a3=ar3: V.tensor_reduce(
                        a3, t["pg"].rearrange("p (i d) -> p i d", d=64),
                        axis=AX.X, op=ALU.add))
                    pg_rd[0] = ev
                att_evts.append(ev)
            for ev in att_evts:
                em.wait("vector", ev)
            arv = t["ar"][:, sl].rearrange("p (r w) -> p w r", w=16)
            mx3 = t["mx"].rearrange("p (w o) -> p w o", o=1)
            e_mx = vop(lambda V, a2=arv, m2=mx3: V.tensor_reduce(a2, m2, axis=AX.X, op=ALU.max) if False else V.tensor_reduce(m2, a2, axis=AX.X, op=ALU.max))
            e_sub = vop(lambda V, s2=sl: V.tensor_tensor(
                t["as"].rearrange("p (r w) -> p r w", w=16),
                t["ar"][:, s2].rearrange("p (r w) -> p r w", w=16),
                t["mx"].rearrange("p (o w) -> p o w", o=1).broadcast_to([128, 16, 16]),
                op=ALU.subtract))
            em.wait("scalar", e_sub)
            e_e = aop(lambda S, s2=sl: S.activation(t["es"][:, s2], t["as"],
                                                    AF.Exp, scale=0.125))
            em.wait("vector", e_e)
            e_dn = vop(lambda V, s2=sl: V.tensor_reduce(
                t["den"].rearrange("p (w o) -> p w o", o=1),
                t["es"][:, s2].rearrange("p (r w) -> p w r", w=16),
                axis=AX.X, op=ALU.add))
            vop(lambda V, s2=sl: V.reciprocal(t["rd"][:, s2], t["den"]))
            e_p = vop(lambda V, s2=sl: V.tensor_tensor(
                t["pr"][:, s2].rearrange("p (r w) -> p r w", w=16),
                t["es"][:, s2].rearrange("p (r w) -> p r w", w=16),
                t["rd"][:, s2].rearrange("p (o w) -> p o w", o=1).broadcast_to([128, 16, 16]),
                op=ALU.mult))
            em.wait("gpsimd", e_p)
            oa_d = t["oa"][:, 0, :]
            oa_g = t["oa"][:, 1, :]
            last_d = None
            last_g = None
            for r in range(16):
                pbc = t["pr"][:, sl, 16 * r:16 * (r + 1)].rearrange(
                    "p (w o) -> p w o", o=1).broadcast_to([128, 16, 64])
                vsl = t["vc"][:, sl, 64 * r:64 * r + 1024].rearrange(
                    "p (w d) -> p w d", d=64)
                if r < 8:
                    if r == 0:
                        last_d = vop(lambda V, pb=pbc, vs=vsl: V.tensor_tensor(
                            oa_d.rearrange("p (w d) -> p w d", d=64), vs, pb, op=ALU.mult))
                    else:
                        vop(lambda V, pb=pbc, vs=vsl: V.tensor_tensor(
                            t["pd"].rearrange("p (w d) -> p w d", d=64), vs, pb, op=ALU.mult))
                        last_d = vop(lambda V: V.tensor_tensor(oa_d, oa_d, t["pd"], op=ALU.add))
                else:
                    if r == 8:
                        last_g = gop(lambda G, pb=pbc, vs=vsl: G.tensor_tensor(
                            oa_g.rearrange("p (w d) -> p w d", d=64), vs, pb, op=ALU.mult))
                    else:
                        gop(lambda G, pb=pbc, vs=vsl: G.tensor_tensor(
                            t["pg"].rearrange("p (w d) -> p w d", d=64), vs, pb, op=ALU.mult))
                        last_g = gop(lambda G: G.tensor_tensor(oa_g, oa_g, t["pg"], op=ALU.add))
            em.wait("vector", last_g)
            em.wait("vector", ob_free[sl])
            e_o = vop(lambda V, s2=sl: V.tensor_tensor(t["ob"][:, s2, :], oa_d, oa_g,
                                                       op=ALU.add))
            em.wait("sync", e_o)
            ospill[ti] = dma("sync", o_s[seq][(ti % 32) * 128:(ti % 32) * 128 + 128, :],
                             t["ob"][:, sl, :], f"so{seq}{sl}")
            x1_free[sl] = e_o
            buf_free[sl] = e_o
            buf_free_g[sl] = last_g
            ob_free[sl] = ospill[ti]
        return ospill

    def alloc_oproj_area():
        em.sboff = pre_off
        t = {}
        t["wo"] = em.sb("o_wo", [128, 8, D], BF16)
        t["o2"] = em.sb("o_o2", [128, 2, D], BF16)
        t["x1t"] = em.sb("o_x1t", [128, 2, D], F32)
        t["ofm"] = em.sb("o_ofm", [128, 2, 8, 128], BF16)
        t["x2st"] = em.sb("o_x2st", [128, 2, D], F32)
        t["x2tmp"] = em.sb("o_x2tmp", [128, D], F32)
        assert em.sboff <= CONST_LIMIT
        return t

    OPP = [(0, 1024), (1024, 2048)]
    OTR = [(2048, 2112), (2560, 2624)]

    def emit_oproj(t, ospill, x2_spills, dbg_name):
        if "pre" in phases:
            em.wait("sync", cast_w.get("o"))
            em.wait("sync", st_sh)
            em.wait("vector", st_sh)
            em.wait("scalar", st_sh)
        e_wo = dma("sync", t["wo"], wo_s, "sWL3")
        em.wait("tensor", e_wo)
        n_tr = [0]
        o2_free = [None, None]
        x1_done = [None, None]
        x2st_free = [None, None]
        for ti in range(NTILE):
            sl = ti % 2
            seq = ti // 32
            al = ti % 32
            head = al // 2
            mh = al % 2
            em.wait("sync", em.evt(f"so{seq}0"))
            em.wait("sync", em.evt(f"so{seq}1"))
            em.wait("sync", o2_free[sl])
            em.wait("sync", x1_done[sl])
            oq = "oq0" if sl == 0 else "oq1"
            oview = o_s[seq].rearrange("(mh m r) (i d) -> mh m r i d",
                                       m=128, r=16, d=64)
            dma("sync", t["o2"][:, sl, :].rearrange("p (r d) -> p r d", d=64),
                oview[mh, :, :, head, :], oq)
            dma("sync", t["x1t"][:, sl], x1_s[ti], oq)
            ldo = ldx = em.evt(oq)
            em.wait("tensor", ldo)
            last_ev = None
            for c in range(8):
                slot = n_tr[0] % 2
                n_tr[0] += 1
                c0, c1 = OTR[slot]
                em.psum_write("tensor", c0, c1)
                trv = psv(c0, c1, bf16=True)
                e_tr = pop(lambda T, c2=c, s2=sl, tv=trv: T.transpose(
                    tv, t["o2"][:, s2, c2 * 128:(c2 + 1) * 128], ident_bf))
                em.psum_write_done("tensor", c0, c1, e_tr)
                em.psum_read("scalar", c0, c1)
                e_ev = aop(lambda S, s2=sl, c2=c, tv=trv: S.copy(
                    t["ofm"][:, s2, c2, :], tv))
                em.psum_read_done("scalar", c0, c1, e_ev)
                last_ev = e_ev
            em.wait("tensor", last_ev)
            c0, c1 = OPP[sl]
            em.psum_write("tensor", c0, c1)
            xv = psv(c0, c1)
            for c in range(8):
                for hf in range(2):
                    e_mm = pop(lambda T, c2=c, h2=hf, s2=sl, x2=xv: T.matmul(
                        x2[:, h2 * 512:(h2 + 1) * 512], t["ofm"][:, s2, c2, :],
                        t["wo"][:, c2, h2 * 512:(h2 + 1) * 512],
                        start=(c2 == 0), stop=(c2 == 7)),
                        milestone=(c == 7 and hf == 1))
            em.psum_write_done("tensor", c0, c1, e_mm)
            em.wait("vector", e_mm)
            em.wait("vector", ldx)
            em.wait("vector", x2st_free[sl])
            em.psum_read("vector", c0, c1)
            vop(lambda V, s2=sl, x2=xv: V.tensor_tensor(
                t["x2tmp"], x2, t["x1t"][:, s2, :], op=ALU.add))
            e_fin = vop(lambda V, s2=sl: V.tensor_tensor(
                t["x2st"][:, s2, :], t["x2tmp"], bo_bc, op=ALU.add))
            em.psum_read_done("vector", c0, c1, e_fin)
            em.wait("sync", e_fin)
            ev_sp = dma("sync", x2_s[ti], t["x2st"][:, sl, :], "sxs0" if sl == 0 else "sxs1")
            if dbg_name is not None:
                ev_sp = dma("sync", dbg_t[dbg_name][ti], t["x2st"][:, sl, :],
                            "sxs0" if sl == 0 else "sxs1")
            x2_spills[ti] = ev_sp
            x2st_free[sl] = ev_sp
            o2_free[sl] = em.evt("sP")
            x1_done[sl] = e_fin

    x1_spills = [None] * NTILE
    x2_spills = [None] * NTILE
    if "p1" in phases:
        t1 = alloc_ffn_area("f_")
        x1_spills = emit_ffn_call(t1, xin, x1_s, True, None,
                                  "x1" if "x1" in dbg else None)
    ospill = [None] * NTILE
    if "p2" in phases:
        t2 = alloc_mix_area()
        ospill = emit_mix(t2, x1_spills)
    if "p3" in phases:
        t3 = alloc_oproj_area()
        emit_oproj(t3, ospill, x2_spills, "x2" if "x2" in dbg else None)
    if "p4" in phases:
        t4 = alloc_ffn_area("z_")
        emit_ffn_call(t4, x2_s, out, False, None,
                      "x3" if "x3" in dbg else None)

    for sname in list(em.cnt):
        if em.cnt[sname]:
            em.emit("sync", lambda E, s=em.sems[sname], v=em.cnt[sname]: E.wait_ge(s, v))

    with nc.Block() as block:
        @block.sync
        def _(E):
            for fn in em.prog["sync"]:
                fn(E)

        @block.gpsimd
        def _(E):
            for fn in em.prog["gpsimd"]:
                fn(E)

        @block.tensor
        def _(E):
            for fn in em.prog["tensor"]:
                fn(E)

        @block.vector
        def _(E):
            for fn in em.prog["vector"]:
                fn(E)

        @block.scalar
        def _(E):
            for fn in em.prog["scalar"]:
                fn(E)

    return nc


def make_in_maps(inputs):
    x = np.asarray(inputs["L0_attn_x"], np.float32)
    shared = dict(
        wqt=np.ascontiguousarray(np.asarray(inputs["L0_attn_Wq"], np.float32).T),
        wkt=np.ascontiguousarray(np.asarray(inputs["L0_attn_Wk"], np.float32).T),
        wvt=np.ascontiguousarray(np.asarray(inputs["L0_attn_Wv"], np.float32).T),
        wot=np.ascontiguousarray(np.asarray(inputs["L0_attn_Wo"], np.float32).T),
        bo=np.ascontiguousarray(np.asarray(inputs["L0_attn_bo"], np.float32).reshape(1, D)),
        ct=np.ascontiguousarray(np.asarray(inputs["L0_ffn_centroids"], np.float32).T),
        eb=np.ascontiguousarray(np.asarray(inputs["L0_ffn_expert_biases"], np.float32).reshape(1, NE)),
        wrt=np.ascontiguousarray(np.asarray(inputs["L0_ffn_Wr"], np.float32).transpose(0, 2, 1)),
        br=np.ascontiguousarray(np.asarray(inputs["L0_ffn_br"], np.float32)),
        wst=np.ascontiguousarray(np.asarray(inputs["L0_ffn_Ws"], np.float32).transpose(0, 2, 1)),
        bs=np.ascontiguousarray(np.asarray(inputs["L0_ffn_bs"], np.float32)),
    )
    maps = []
    for r in range(NCORES):
        m = dict(shared)
        m["xin"] = np.ascontiguousarray(x[2 * r:2 * r + 2].reshape(NTILE, 128, D))
        maps.append(m)
    return maps


def kernel(**inputs):
    in_maps = make_in_maps(inputs)
    nc = build_program()
    res = run_bass_kernel_spmd(nc, in_maps, core_ids=list(range(NCORES)))
    outs = [np.asarray(r["out"], np.float32).reshape(2, SEQ, D) for r in res.results]
    return np.concatenate(outs, axis=0)


# version 02
# speedup vs baseline: 1.1815x; 1.1815x over previous
from collections import deque

import numpy as np
import concourse.bass as bass
import concourse.mybir as mybir
from concourse.bass_utils import run_bass_kernel_spmd

F32 = mybir.dt.float32
BF16 = mybir.dt.bfloat16
AF = mybir.ActivationFunctionType
ALU = mybir.AluOpType
AX = mybir.AxisListType

NCORES = 8
SEQ = 4096
D = 1024
NH = 16
NE = 16
TOK = 2 * SEQ
NTILE = TOK // 128
GT = 8
NGRP = NTILE // GT
EPS = 1e-5

SBUF_BASE = 17408
SBUF_TOP = 229344

PP = [(0, 1024), (1024, 2048), (2048, 3072)]
TRB = [(3072, 3136), (3584, 3648)]
PSR = [(3200, 3216), (3712, 3728)]
MQ, MK, MV = (0, 1024), (1024, 2048), (2048, 3072)


class Emitter:
    ENGS = ("sync", "gpsimd", "tensor", "vector", "scalar")

    def __init__(self, nc):
        self.nc = nc
        self.prog = {e: [] for e in self.ENGS}
        self.sems = {}
        self.cnt = {}
        self.hw = {e: {} for e in self.ENGS}
        self.sboff = SBUF_BASE
        self.bank = {b: {"w": None, "r": []} for b in range(8)}

    def sb(self, name, shape, dtype):
        per = int(np.prod(shape[1:])) * mybir.dt.size(dtype)
        off = (self.sboff + 31) // 32 * 32
        self.sboff = off + per
        return self.nc.alloc_sbuf_tensor_at(name, shape, dtype, offset=off).ap()

    def sb_at(self, name, shape, dtype, off):
        return self.nc.alloc_sbuf_tensor_at(name, shape, dtype, offset=off).ap()

    def sem(self, name):
        h = self.nc.alloc_semaphore(name=name)
        self.sems[name] = h
        self.cnt[name] = 0
        return h

    def inc(self, name, by=1):
        self.cnt[name] += by
        return (name, self.cnt[name])

    def evt(self, name):
        return (name, self.cnt[name])

    def emit(self, eng, fn):
        self.prog[eng].append(fn)

    def wait(self, eng, evt):
        if evt is None:
            return
        sname, val = evt
        if val <= 0 or self.hw[eng].get(sname, 0) >= val:
            return
        self.hw[eng][sname] = val
        sem = self.sems[sname]
        self.emit(eng, lambda E, s=sem, v=val: E.wait_ge(s, v))

    def _banks(self, c0, c1):
        return range(c0 // 512, (c1 - 1) // 512 + 1)

    def psum_write(self, eng, c0, c1):
        for b in self._banks(c0, c1):
            st = self.bank[b]
            for (s, c, e) in st["r"]:
                if e != eng:
                    self.wait(eng, (s, c))
            if st["w"] is not None and st["w"][2] != eng:
                self.wait(eng, (st["w"][0], st["w"][1]))

    def psum_write_done(self, eng, c0, c1, evt):
        for b in self._banks(c0, c1):
            self.bank[b] = {"w": (evt[0], evt[1], eng), "r": []}

    def psum_read(self, eng, c0, c1):
        for b in self._banks(c0, c1):
            st = self.bank[b]
            if st["w"] is not None and st["w"][2] != eng:
                self.wait(eng, (st["w"][0], st["w"][1]))

    def psum_read_done(self, eng, c0, c1, evt):
        for b in self._banks(c0, c1):
            self.bank[b]["r"].append((evt[0], evt[1], eng))


def build_program(phases=("pre", "p1", "p2", "p3", "p4"), dbg=(), paranoid=False, triv_ln=False):
    nc = bass.Bass("TRN2", target_bir_lowering=False)
    em = Emitter(nc)

    xin = nc.dram_tensor("xin", [NTILE, 128, D], F32, kind="ExternalInput").ap()
    wqt = nc.dram_tensor("wqt", [D, D], F32, kind="ExternalInput").ap()
    wkt = nc.dram_tensor("wkt", [D, D], F32, kind="ExternalInput").ap()
    wvt = nc.dram_tensor("wvt", [D, D], F32, kind="ExternalInput").ap()
    wot = nc.dram_tensor("wot", [D, D], F32, kind="ExternalInput").ap()
    bo_in = nc.dram_tensor("bo", [1, D], F32, kind="ExternalInput").ap()
    ct_in = nc.dram_tensor("ct", [D, NE], F32, kind="ExternalInput").ap()
    eb_in = nc.dram_tensor("eb", [1, NE], F32, kind="ExternalInput").ap()
    wrt_in = nc.dram_tensor("wrt", [NE, D, D], F32, kind="ExternalInput").ap()
    br_in = nc.dram_tensor("br", [NE, D], F32, kind="ExternalInput").ap()
    wst_in = nc.dram_tensor("wst", [2, D, D], F32, kind="ExternalInput").ap()
    bs_in = nc.dram_tensor("bs", [2, D], F32, kind="ExternalInput").ap()
    out = nc.dram_tensor("out", [NTILE, 128, D], F32, kind="ExternalOutput").ap()

    wr_bf = nc.dram_tensor("wr_bf", [NE, 128, 8, D], BF16, kind="Internal").ap()
    wsh_s = nc.dram_tensor("wsh_s", [128, 8, D], BF16, kind="Internal").ap()
    wq_s = nc.dram_tensor("wq_s", [128, 8, D], BF16, kind="Internal").ap()
    wk_s = nc.dram_tensor("wk_s", [128, 8, D], BF16, kind="Internal").ap()
    wv_s = nc.dram_tensor("wv_s", [128, 8, D], BF16, kind="Internal").ap()
    wo_s = nc.dram_tensor("wo_s", [128, 8, D], BF16, kind="Internal").ap()
    x1_s = nc.dram_tensor("x1_s", [NTILE, 128, D], F32, kind="Internal").ap()
    bo_s = nc.dram_tensor("bo_s", [128, D], F32, kind="Internal").ap()
    x2_s = nc.dram_tensor("x2_s", [NTILE, 128, D], F32, kind="Internal").ap()
    o_s = nc.dram_tensor("o_s", [2, SEQ, D], BF16, kind="Internal").ap()

    dbg_t = {}
    for name in dbg:
        dbg_t[name] = nc.dram_tensor("dbg_" + name, [NTILE, 128, D], F32,
                                     kind="ExternalOutput").ap()

    ctop = SBUF_TOP

    def const_sb(name, shape, dtype):
        nonlocal ctop
        per = int(np.prod(shape[1:])) * mybir.dt.size(dtype)
        ctop = (ctop - per) // 32 * 32
        return em.sb_at(name, shape, dtype, ctop)

    ident_bf = const_sb("ident_bf", [128, 128], BF16)
    ones_bf = const_sb("ones_bf", [1, 128], BF16)
    ones_f = const_sb("ones_f", [1, 128], F32)
    ct_bf = const_sb("ct_bf", [128, 8, NE], BF16)
    brh_bf = const_sb("brh_bf", [NE, D], BF16)
    bsh4_bf = const_sb("bsh4_bf", [1, D], BF16)
    eb_bc = const_sb("eb_bc", [128, NE], F32)
    CONST_LIMIT = ctop

    psa = nc.alloc_psum_tensor("psarena", [128, 4096], F32).ap()

    def psv(c0, c1, bf16=False):
        v = psa[:, c0:c1]
        if bf16:
            v = v.bitcast(BF16)
        return v

    DMA_SEM_NAMES = ("sQ", "sGd", "sCast", "sPre", "sWsh", "sWL2", "sWL3",
                     "sx0", "sx1", "ss0", "ss1", "wb0", "wb1",
                     "so00", "so01", "so10", "so11", "oq0", "oq1", "sxs0", "sxs1", "sx1t0", "sx1t1",
                     "sDbg")
    for s in ("sP", "sV", "sA", "sGc") + DMA_SEM_NAMES:
        em.sem(s)

    def barrier_all(eng):
        if not paranoid:
            return
        for sn in ("sP", "sV", "sA", "sGc", "sGd", "sQ"):
            em.wait(eng, em.evt(sn))

    def vop(make):
        barrier_all("vector")
        evt = em.inc("sV")
        prev = evt[1] - 1
        sem = em.sems["sV"]
        def fn(V, m=make, s=sem, p=prev):
            if p > 0:
                V.wait_ge(s, p)
            m(V).then_inc(s, 1)
        em.emit("vector", fn)
        return evt

    def aop(make):
        barrier_all("scalar")
        evt = em.inc("sA")
        prev = evt[1] - 1
        sem = em.sems["sA"]
        def fn(S, m=make, s=sem, p=prev):
            if p > 0:
                S.wait_ge(s, p)
            m(S).then_inc(s, 1)
        em.emit("scalar", fn)
        return evt

    def gop(make):
        barrier_all("gpsimd")
        evt = em.inc("sGc")
        prev = evt[1] - 1
        sem = em.sems["sGc"]
        def fn(G, m=make, s=sem, p=prev):
            if p > 0:
                G.wait_ge(s, p)
            m(G).then_inc(s, 1)
        em.emit("gpsimd", fn)
        return evt

    def pop(make, milestone=True):
        barrier_all("tensor")
        if milestone:
            evt = em.inc("sP")
            sem = em.sems["sP"]
            em.emit("tensor", lambda T, m=make, s=sem: m(T).then_inc(s, 1))
            return evt
        if paranoid:
            evt = em.inc("sP")
            sem = em.sems["sP"]
            em.emit("tensor", lambda T, m=make, s2=sem: m(T).then_inc(s2, 1))
            return evt
        em.emit("tensor", lambda T, m=make: m(T))
        return None

    def dma(queue, outap, inap, sem_name):
        barrier_all(queue)
        evt = em.inc(sem_name, 16)
        sem = em.sems[sem_name]
        em.emit(queue, lambda E, o=outap, i=inap, s=sem:
                E.dma_start(out=o, in_=i).then_inc(s, 16))
        return evt

    pre_off = em.sboff
    pw0 = em.sb("pw0", [128, 8, D], F32)
    pw1 = em.sb("pw1", [128, 8, D], F32)
    pst = em.sb("pst", [NE, D], F32)
    pbs = em.sb("pbs", [1, 2 * D], F32)
    pbo = em.sb("pbo", [1, D], F32)
    peb = em.sb("peb", [1, NE], F32)
    psh_bf = em.sb("psh_bf", [128, 8, D], BF16)

    cast_evts = {e: None for e in range(NE)}
    cast_w = {}
    e_brh = e_bsh = st_bo = None
    st_sh = None
    pre_v_done = None
    pre_p_done = None
    pre_q_done = None

    if "pre" in phases:
        gop(lambda G: G.memset(ident_bf, 0.0))
        gop(lambda G: G.affine_select(out=ident_bf, in_=ident_bf,
                                      compare_op=ALU.not_equal, fill=1.0, base=0,
                                      pattern=[[-1, 128]], channel_multiplier=1))
        gop(lambda G: G.memset(ones_bf, 1.0))
        gop(lambda G: G.memset(ones_f, 1.0))
        e_gc = em.evt("sGc")
        em.wait("tensor", e_gc)
        em.wait("scalar", e_gc)
        em.wait("vector", e_gc)
        cast_w["ct"] = dma("gpsimd", ct_bf, ct_in.rearrange("(c p) e -> p c e", p=128), "sCast")
        for e in range(NE):
            cast_evts[e] = dma("gpsimd", wr_bf[e],
                               wrt_in[e].rearrange("(c p) o -> p c o", p=128), "sCast")
        cast_w["q"] = dma("gpsimd", wq_s, wqt.rearrange("(c p) o -> p c o", p=128), "sCast")
        cast_w["k"] = dma("gpsimd", wk_s, wkt.rearrange("(c p) o -> p c o", p=128), "sCast")
        cast_w["v"] = dma("gpsimd", wv_s, wvt.rearrange("(c p) o -> p c o", p=128), "sCast")
        cast_w["o"] = dma("gpsimd", wo_s, wot.rearrange("(c p) o -> p c o", p=128), "sCast")
        cast_all = em.evt("sCast")
        cast_w = {k: cast_all for k in cast_w}
        cast_evts = {e: cast_all for e in cast_evts}

        ld_w0 = dma("sync", pw0, wst_in[0].rearrange("(c p) o -> p c o", p=128), "sPre")
        ld_w1 = dma("sync", pw1, wst_in[1].rearrange("(c p) o -> p c o", p=128), "sPre")
        ld_br = dma("sync", pst, br_in, "sPre")
        ld_bs = dma("sync", pbs, bs_in.rearrange("a d -> (a d)").unsqueeze(0), "sPre")
        ld_bo = dma("sync", pbo, bo_in, "sPre")
        ld_eb = dma("sync", peb, eb_in, "sPre")

        pre_batch = em.evt("sPre")
        em.wait("vector", pre_batch)
        vop(lambda V: V.tensor_tensor(pw0.rearrange("p c o -> p (c o)"),
                                      pw0.rearrange("p c o -> p (c o)"),
                                      pw1.rearrange("p c o -> p (c o)"), op=ALU.add))
        e_sh = vop(lambda V: V.tensor_scalar(psh_bf.rearrange("p c o -> p (c o)"),
                                             pw0.rearrange("p c o -> p (c o)"),
                                             0.5, None, op0=ALU.mult))
        em.wait("sync", e_sh)
        st_sh = dma("sync", wsh_s, psh_bf, "sWsh")

        e_brh = vop(lambda V: V.tensor_scalar(brh_bf, pst, 0.5, None, op0=ALU.mult))
        vop(lambda V: V.tensor_tensor(pbs[0:1, 0:D], pbs[0:1, 0:D], pbs[0:1, D:2 * D], op=ALU.add))
        e_bsh = vop(lambda V: V.tensor_scalar(bsh4_bf, pbs[0:1, 0:D], 0.25, None, op0=ALU.mult))

        em.wait("tensor", pre_batch)
        em.psum_write("tensor", 2048, 2064)
        e_mm = pop(lambda T: T.matmul(psv(2048, 2064), ones_f, peb, start=True, stop=True))
        em.psum_write_done("tensor", 2048, 2064, e_mm)
        em.psum_read("vector", 2048, 2064)
        e_ebc = vop(lambda V: V.tensor_copy(eb_bc, psv(2048, 2064)))
        em.psum_read_done("vector", 2048, 2064, e_ebc)

        em.wait("tensor", pre_batch)
        em.psum_write("tensor", 0, 1024)
        pop(lambda T: T.matmul(psv(0, 512), ones_f, pbo[:, 0:512], start=True, stop=True),
            milestone=False)
        e_mm2 = pop(lambda T: T.matmul(psv(512, 1024), ones_f, pbo[:, 512:1024],
                                       start=True, stop=True))
        em.psum_write_done("tensor", 0, 1024, e_mm2)
        em.psum_read("vector", 0, 1024)
        e_boc = vop(lambda V: V.tensor_copy(pw0[:, 0, :], psv(0, 1024)))
        em.psum_read_done("vector", 0, 1024, e_boc)
        em.wait("sync", e_boc)
        st_bo = dma("sync", bo_s, pw0[:, 0, :], "sQ")

        pre_v_done = em.evt("sV")
        pre_p_done = em.evt("sP")

    def alloc_ffn_area(px):
        em.sboff = pre_off
        t = {}
        t["u_fm"] = em.sb(px + "ufm", [128, 8, GT * 128], BF16)
        t["acc"] = em.sb(px + "acc", [128, GT, D], F32)
        t["wbuf"] = em.sb(px + "wbuf", [128, 2, 8, D], BF16)
        t["uh"] = em.sb(px + "uh", [128, 2, D], BF16)
        t["xt"] = em.sb(px + "xt", [128, 2, D], F32)
        t["xm"] = em.sb(px + "xm", [128, D], F32)
        t["g"] = em.sb(px + "g", [128, GT, NE], F32)
        t["gbf"] = em.sb(px + "gbf", [128, NE], BF16)
        t["gt"] = em.sb(px + "gt", [16, GT, 128], BF16)
        t["rsc"] = em.sb(px + "rsc", [128, 128], F32)
        assert em.sboff <= CONST_LIMIT, f"FFN area overflow {em.sboff}"
        return t

    def emit_ln(t, src_ap, uh_dst, halved, ln_deps, ld_evt, sqd_ap,
                acc_dst=None, acc_deps=()):
        rs = t["rsc"]
        msum, mneg, vsum, svar, stdv, rstd, rstd_h = (
            rs[:, k:k + 1] for k in range(7))
        xm = t["xm"]
        for dep in ln_deps:
            em.wait("vector", dep)
        for dep in acc_deps:
            em.wait("vector", dep)
        em.wait("vector", ld_evt)
        vop(lambda V: V.tensor_reduce(msum, src_ap, axis=AX.X, op=ALU.add))
        vop(lambda V: V.tensor_scalar(mneg, msum, -1.0 / D, None, op0=ALU.mult))
        e_xm = vop(lambda V: V.tensor_scalar(xm, src_ap, mneg, None, op0=ALU.add))
        e_sq = vop(lambda V: V.scalar_tensor_tensor(sqd_ap, xm, 1.0, xm,
                                                    op0=ALU.mult, op1=ALU.mult,
                                                    accum_out=vsum))
        e_vs = vop(lambda V: V.tensor_scalar(svar, vsum, 1.0 / D, EPS,
                                             op0=ALU.mult, op1=ALU.add))
        em.wait("scalar", e_vs)
        aop(lambda S: S.activation(stdv, svar, AF.Ln))
        e_st = aop(lambda S: S.activation(rstd, stdv, AF.Exp, scale=-0.5))
        em.wait("vector", e_st)
        ruse = rstd
        if halved:
            vop(lambda V: V.tensor_scalar(rstd_h, rstd, 0.5, None, op0=ALU.mult))
            ruse = rstd_h
        e_uh = vop(lambda V, r=ruse, ud=uh_dst: V.tensor_scalar(
            ud, xm, r, None, op0=ALU.mult))
        e_acc = None
        if acc_dst is not None:
            e_acc = vop(lambda V, a=acc_dst, s=src_ap: V.scalar_tensor_tensor(
                a, xm, rstd_h, s, op0=ALU.mult, op1=ALU.add))
        return e_uh, e_sq, e_acc

    def emit_transposes(src_full, dst_fn, e_src, n_tr, free_evt=None):
        em.wait("tensor", e_src)
        last = None
        for c in range(8):
            slot = n_tr[0] % 2
            n_tr[0] += 1
            c0, c1 = TRB[slot]
            em.psum_write("tensor", c0, c1)
            trv = psv(c0, c1, bf16=True)
            e_tr = pop(lambda T, c2=c, tv=trv, s=src_full: T.transpose(
                tv, s[:, c2 * 128:(c2 + 1) * 128], ident_bf))
            em.psum_write_done("tensor", c0, c1, e_tr)
            em.psum_read("scalar", c0, c1)
            if c == 0 and free_evt is not None:
                em.wait("scalar", free_evt)
            e_ev = aop(lambda S, d=dst_fn(c), tv=trv: S.copy(d, tv))
            em.psum_read_done("scalar", c0, c1, e_ev)
            last = e_ev
        return last

    def emit_router(t, i, e_ufm, n_r):
        slot = n_r[0] % 2
        n_r[0] += 1
        c0, c1 = PSR[slot]
        prv = psv(c0, c1)
        em.wait("tensor", e_ufm)
        em.wait("tensor", cast_w.get("ct"))
        em.psum_write("tensor", c0, c1)
        for c in range(8):
            e_mm = pop(lambda T, c2=c: T.matmul(
                prv, t["u_fm"][:, c2, i * 128:(i + 1) * 128],
                ct_bf[:, c2, :], start=(c2 == 0), stop=(c2 == 7)),
                milestone=(c == 7))
        em.psum_write_done("tensor", c0, c1, e_mm)
        rs = t["rsc"]
        scf, e_r, s_r = rs[:, 16:32], rs[:, 32:48], rs[:, 48:64]
        msk, mskd = rs[:, 64:80], rs[:, 80:96]
        mn, den, rden = rs[:, 96:97], rs[:, 97:98], rs[:, 98:99]
        m1, m2 = rs[:, 99:100], rs[:, 100:101]
        em.psum_read("vector", c0, c1)
        e_sc = vop(lambda V: V.scalar_tensor_tensor(scf, prv, 2.0, eb_bc,
                                                    op0=ALU.mult, op1=ALU.add))
        em.psum_read_done("vector", c0, c1, e_sc)
        e_mn = vop(lambda V: V.tensor_reduce(mn, scf, axis=AX.X, op=ALU.max, negate=True))
        em.wait("scalar", e_mn)
        e_exp = aop(lambda S: S.activation(e_r, scf, AF.Exp, bias=mn, scale=1.0,
                                           accum_out=den))
        em.wait("vector", e_exp)
        vop(lambda V: V.reciprocal(rden, den))
        vop(lambda V: V.tensor_scalar(s_r, e_r, rden, None, op0=ALU.mult))
        vop(lambda V: V.tensor_reduce(m1, s_r, axis=AX.X, op=ALU.max))
        vop(lambda V: V.tensor_scalar(msk, s_r, m1, None, op0=ALU.is_ge))
        vop(lambda V: V.scalar_tensor_tensor(mskd, msk, -1e9, s_r, op0=ALU.mult, op1=ALU.add))
        vop(lambda V: V.tensor_reduce(m2, mskd, axis=AX.X, op=ALU.max))
        vop(lambda V: V.tensor_scalar(msk, s_r, m2, None, op0=ALU.is_ge))
        e_g = vop(lambda V: V.tensor_tensor(t["g"][:, i, :], s_r, msk, op=ALU.mult))
        e_gb = vop(lambda V: V.tensor_copy(t["gbf"], t["g"][:, i, :]))
        return e_g, e_gb

    def emit_gtr(t, i, e_gb, n_tr):
        em.wait("tensor", e_gb)
        slot = n_tr[0] % 2
        n_tr[0] += 1
        c0, c1 = TRB[slot]
        gv = psv(c0, c1, bf16=True)[0:16, :]
        em.psum_write("tensor", c0, c1)
        e_tr = pop(lambda T: T.transpose(gv, t["gbf"], ident_bf))
        em.psum_write_done("tensor", c0, c1, e_tr)
        em.psum_read("vector", c0, c1)
        e_ev = vop(lambda V: V.tensor_copy(t["gt"][:, i, :], gv))
        em.psum_read_done("vector", c0, c1, e_ev)
        return e_ev

    def emit_ffn_call(t, src_dram, dst_dram, first_call, dbg_name,
                      pre_group_hook=None, post_group_hook=None,
                      expert_hook=None, acc_guard=(),
                      group_start_hook=None, evac_hook=None):
        n_tr, n_r, n_pp = [0], [0], [0]
        ufm_ready = [None]
        g_ready = [None] * GT
        gt_ready = [None] * GT
        uh_free = [None] * GT
        wb_free = [None, None]
        acc_done = [None] * GT
        spills = [None] * NTILE
        spill_batch = [None, None]
        xt_state = {"n": 0, "free": [None, None], "gate": ()}

        def xt_load(ti):
            sl = xt_state["n"] % 2
            xt_state["n"] += 1
            for gv in xt_state["gate"]:
                em.wait("sync", gv)
            em.wait("sync", xt_state["free"][sl])
            ev = dma("sync", t["xt"][:, sl], src_dram[ti], "sx0" if sl == 0 else "sx1")
            return sl, ev

        def stage_a(g):
            last_ufm = None
            if g > 0:
                adeps = [spill_batch[(g - 1) % 2]]
            else:
                adeps = list(acc_guard)
            for i in range(GT):
                ti = g * GT + i
                sl, ev = xt_load(ti)
                usl = i % 2
                deps = [uh_free[usl]] if uh_free[usl] is not None else []
                e_uh, _, e_acc = emit_ln(t, t["xt"][:, sl], t["uh"][:, usl, :],
                                         True, deps, ev, t["acc"][:, i, :],
                                         acc_dst=t["acc"][:, i, :],
                                         acc_deps=adeps)
                xt_state["free"][sl] = e_acc
                dst_fn = lambda c, i2=i: t["u_fm"][:, c, i2 * 128:(i2 + 1) * 128]
                e_ev = emit_transposes(t["uh"][:, usl, :], dst_fn, e_uh, n_tr)
                uh_free[usl] = e_ev
                last_ufm = e_ev
                e_g, e_gb = emit_router(t, i, e_ev, n_r)
                g_ready[i] = e_g
                gt_ready[i] = emit_gtr(t, i, e_gb, n_tr)
            ufm_ready[0] = last_ufm

        def stage_b(g):
            parity = g % 2
            if group_start_hook:
                group_start_hook(g)
            w_evts = []

            def emit_stream(e):
                src = wr_bf[e] if e < 16 else wsh_s
                slot = e % 2
                em.wait("gpsimd", wb_free[slot])
                if "pre" in phases:
                    em.wait("gpsimd", em.evt("sCast"))
                    em.wait("gpsimd", st_sh)
                w_evts.append(dma("gpsimd", t["wbuf"][:, slot], src,
                                  "wb0" if slot == 0 else "wb1"))

            emit_stream(0)
            emit_stream(1)
            for e in range(17):
                slot = e % 2
                em.wait("tensor", w_evts[e])
                em.wait("tensor", ufm_ready[0])
                for i in range(GT):
                    pslot = n_pp[0] % len(PP)
                    n_pp[0] += 1
                    c0, c1 = PP[pslot]
                    ppv = psv(c0, c1)
                    em.psum_write("tensor", c0, c1)
                    ev = None
                    if e == 16:
                        em.wait("tensor", gt_ready[i])
                        em.wait("tensor", e_brh)
                        em.wait("tensor", e_bsh)
                        for h in range(2):
                            pop(lambda T, i2=i, h2=h, pv=ppv: T.matmul(
                                pv[:, h2 * 512:(h2 + 1) * 512],
                                t["gt"][:, i2, :],
                                brh_bf[:, h2 * 512:(h2 + 1) * 512],
                                start=True, stop=False), milestone=False)
                            pop(lambda T, h2=h, pv=ppv: T.matmul(
                                pv[:, h2 * 512:(h2 + 1) * 512], ones_bf,
                                bsh4_bf[:, h2 * 512:(h2 + 1) * 512],
                                start=False, stop=False), milestone=False)
                    for c in range(8):
                        for h in range(2):
                            ml = (c == 7 and h == 1)
                            ev2 = pop(lambda T, i2=i, c2=c, h2=h, pv=ppv, sl2=slot,
                                      e2=e: T.matmul(
                                pv[:, h2 * 512:(h2 + 1) * 512],
                                t["u_fm"][:, c2, i2 * 128:(i2 + 1) * 128],
                                t["wbuf"][:, sl2, c2, h2 * 512:(h2 + 1) * 512],
                                start=(c2 == 0 and e2 != 16), stop=(c2 == 7)),
                                milestone=ml)
                            if ml:
                                ev = ev2
                    em.psum_write_done("tensor", c0, c1, ev)
                    em.wait("vector", ev)
                    em.psum_read("vector", c0, c1)
                    if e == 16:
                        e_evac = vop(lambda V, i2=i, pv=ppv: V.tensor_tensor(
                            t["acc"][:, i2, :], t["acc"][:, i2, :], pv, op=ALU.add))
                    else:
                        em.wait("vector", g_ready[i])
                        gcol = t["g"][:, i, e:e + 1]
                        e_evac = vop(lambda V, i2=i, gc=gcol, pv=ppv: V.scalar_tensor_tensor(
                            t["acc"][:, i2, :], pv, gc, t["acc"][:, i2, :],
                            op0=ALU.mult, op1=ALU.add))
                    em.psum_read_done("vector", c0, c1, e_evac)
                    if e == 16:
                        acc_done[i] = e_evac
                    if evac_hook:
                        evac_hook()
                wb_free[slot] = em.evt("sP")
                if e + 2 <= 16:
                    emit_stream(e + 2)
                if expert_hook:
                    expert_hook(g, e)
            ssem = "ss0" if parity == 0 else "ss1"
            for i in range(GT):
                ti = g * GT + i
                em.wait("sync", acc_done[i])
                spills[ti] = dma("sync", dst_dram[ti], t["acc"][:, i, :], ssem)
                if dbg_name is not None:
                    spills[ti] = dma("sync", dbg_t[dbg_name][ti], t["acc"][:, i, :], ssem)
            spill_batch[parity] = em.evt(ssem)

        if "pre" in phases:
            em.wait("sync", pre_v_done)
            em.wait("sync", pre_p_done)
            em.wait("sync", st_sh)
            em.wait("vector", st_sh)
            em.wait("sync", st_bo)
            em.wait("vector", st_bo)
            em.wait("scalar", st_bo)
            em.wait("scalar", st_sh)
            em.wait("gpsimd", pre_v_done)
            em.wait("gpsimd", st_sh)
            em.wait("gpsimd", st_bo)
        if pre_group_hook:
            pre_group_hook(0)
            xt_state["gate"] = (em.evt("sxs0"), em.evt("sxs1"))
        stage_a(0)
        for g in range(NGRP):
            stage_b(g)
            if g + 1 < NGRP:
                if pre_group_hook:
                    pre_group_hook(g + 1)
                    xt_state["gate"] = (em.evt("sxs0"), em.evt("sxs1"))
                stage_a(g + 1)
            if post_group_hook:
                post_group_hook(g)
        return spills

    NROT_DVE = 16
    NAV_DVE = 5

    def make_mix(ffn_end_off):
        em.sboff = ffn_end_off
        t = {}
        t["wq"] = em.sb("m_wq", [128, 8, D], BF16)
        t["wk"] = em.sb("m_wk", [128, 8, D], BF16)
        t["wv"] = em.sb("m_wv", [128, 8, D], BF16)
        t["x1t"] = em.sb("m_x1t", [128, D], F32)
        t["uhm"] = em.sb("m_uhm", [128, D], BF16)
        t["yfm"] = em.sb("m_yfm", [128, 8, 128], BF16)
        t["qc"] = em.sb("m_qc", [128, 1024], F32)
        t["kc"] = em.sb("m_kc", [128, 2048], F32)
        t["vc"] = em.sb("m_vc", [128, 2048], F32)
        t["ar"] = em.sb("m_ar", [128, 256], F32)
        t["pr"] = em.sb("m_pr", [128, 256], F32)
        t["mx"] = em.sb("m_mx", [128, 16], F32)
        t["den"] = em.sb("m_den", [128, 16], F32)
        t["rd"] = em.sb("m_rd", [128, 16], F32)
        t["pd"] = em.sb("m_pd", [128, 1024], F32)
        t["pg"] = em.sb("m_pg", [128, 2, 1024], F32)
        t["oa"] = em.sb("m_oa", [128, 2, D], F32)
        mix_end = em.sboff
        assert mix_end <= CONST_LIMIT, f"mix overflow {mix_end}"

        st = {"x1_free": None, "uh_free": None,
              "qk_free": (), "v_free": (), "oa_free": None,
              "n_tr": [0], "loaded_w": False, "qkv_free": None}
        ffn_t = [None]

        def mix_tile_gen(ti, x1_ready, spills_out):
            seq = ti // 32
            if not st["loaded_w"]:
                st["loaded_w"] = True
                em.wait("sync", cast_w.get("v"))
                dma("sync", t["wq"], wq_s, "sWL2")
                dma("sync", t["wk"], wk_s, "sWL2")
                dma("sync", t["wv"], wv_s, "sWL2")
                st["wl"] = em.evt("sWL2")
            em.wait("sync", st["x1_free"])
            em.wait("sync", x1_ready)
            ldx = dma("sync", t["x1t"], x1_s[ti], "sx1t0")
            yield
            tshare = dict(t)
            tshare["xm"] = ffn_t[0]["xm"]
            tshare["rsc"] = ffn_t[0]["rsc"]
            deps = [st["uh_free"]] if st["uh_free"] is not None else []
            e_uh, e_sq, _ = emit_ln(tshare, t["x1t"], t["uhm"], False, deps,
                                    ldx, t["x1t"])
            st["x1_free"] = e_sq
            yield
            dst_fn = lambda c: t["yfm"][:, c, :]
            e_yfm = emit_transposes(t["uhm"], dst_fn, e_uh, st["n_tr"],
                                    free_evt=st["qkv_free"])
            st["uh_free"] = e_yfm
            em.wait("tensor", e_yfm)
            em.wait("tensor", st["wl"])
            for rng in (MQ, MK, MV):
                em.psum_write("tensor", rng[0], rng[1])
            e_q = e_k = e_v = None
            for c in range(8):
                lhs = t["yfm"][:, c, :]
                for w, rng in (("wq", MQ), ("wk", MK), ("wv", MV)):
                    for h in range(2):
                        ml = (c == 7 and h == 1)
                        ev = pop(lambda T, l=lhs, w2=w, c2=c, h2=h, r=rng: T.matmul(
                            psv(r[0] + h2 * 512, r[0] + (h2 + 1) * 512), l,
                            t[w2][:, c2, h2 * 512:(h2 + 1) * 512],
                            start=(c2 == 0), stop=(c2 == 7)), milestone=ml)
                        if ml:
                            if w == "wq":
                                e_q = ev
                            elif w == "wk":
                                e_k = ev
                            else:
                                e_v = ev
            st["qkv_free"] = e_v
            em.psum_write_done("tensor", MQ[0], MQ[1], e_q)
            em.psum_write_done("tensor", MK[0], MK[1], e_k)
            em.psum_write_done("tensor", MV[0], MV[1], e_v)
            for evv in st["qk_free"]:
                em.wait("scalar", evv)
            em.wait("scalar", e_q)
            em.psum_read("scalar", MQ[0], MQ[1])
            e1 = aop(lambda S: S.copy(t["qc"], psv(*MQ)))
            em.psum_read_done("scalar", MQ[0], MQ[1], e1)
            em.wait("scalar", e_k)
            em.psum_read("scalar", MK[0], MK[1])
            e2 = aop(lambda S: S.copy(t["kc"][:, 0:1024], psv(*MK)))
            em.psum_read_done("scalar", MK[0], MK[1], e2)
            e2d = aop(lambda S: S.copy(t["kc"][:, 1024:2048], t["kc"][:, 0:1024]))
            for evv in st["v_free"]:
                em.wait("scalar", evv)
            em.wait("scalar", e_v)
            em.psum_read("scalar", MV[0], MV[1])
            e3 = aop(lambda S: S.copy(t["vc"][:, 0:1024], psv(*MV)))
            em.psum_read_done("scalar", MV[0], MV[1], e3)
            e3d = aop(lambda S: S.copy(t["vc"][:, 1024:2048], t["vc"][:, 0:1024]))
            yield
            em.wait("vector", e2d)
            em.wait("gpsimd", e2d)
            att_evts = []
            pg_rd = [None, None]
            e_mv_last = e_mg_last = None
            ng = 0
            for r in range(16):
                ar3 = t["ar"][:, 16 * r:16 * (r + 1)].rearrange("p (i o) -> p i o", o=1)
                if r < NROT_DVE:
                    e_mv_last = vop(lambda V, r2=r: V.tensor_tensor(
                        t["pd"], t["qc"],
                        t["kc"][:, 64 * r2:64 * r2 + 1024], op=ALU.mult))
                    ev = vop(lambda V, a3=ar3: V.tensor_reduce(
                        a3, t["pd"].rearrange("p (i d) -> p i d", d=64),
                        axis=AX.X, op=ALU.add))
                else:
                    psl = ng % 2
                    ng += 1
                    em.wait("gpsimd", pg_rd[psl])
                    e_mg_last = gop(lambda G, r2=r, s2=psl: G.tensor_tensor(
                        t["pg"][:, s2, :], t["qc"],
                        t["kc"][:, 64 * r2:64 * r2 + 1024], op=ALU.mult))
                    em.wait("vector", e_mg_last)
                    ev = vop(lambda V, a3=ar3, s2=psl: V.tensor_reduce(
                        a3, t["pg"][:, s2, :].rearrange("p (i d) -> p i d", d=64),
                        axis=AX.X, op=ALU.add))
                    pg_rd[psl] = ev
                att_evts.append(ev)
                yield
            st["qk_free"] = (e_mv_last, e_mg_last)
            for ev in att_evts:
                em.wait("vector", ev)
            arv = t["ar"].rearrange("p (r w) -> p w r", w=16)
            mx3 = t["mx"].rearrange("p (w o) -> p w o", o=1)
            vop(lambda V, a2=arv, m2=mx3: V.tensor_reduce(m2, a2, axis=AX.X, op=ALU.max))
            e_sub = vop(lambda V: V.tensor_tensor(
                t["ar"].rearrange("p (r w) -> p r w", w=16),
                t["ar"].rearrange("p (r w) -> p r w", w=16),
                t["mx"].rearrange("p (o w) -> p o w", o=1).broadcast_to([128, 16, 16]),
                op=ALU.subtract))
            em.wait("scalar", e_sub)
            e_e = aop(lambda S: S.activation(t["pr"], t["ar"], AF.Exp, scale=0.125))
            em.wait("vector", e_e)
            vop(lambda V: V.tensor_reduce(
                t["den"].rearrange("p (w o) -> p w o", o=1),
                t["pr"].rearrange("p (r w) -> p w r", w=16),
                axis=AX.X, op=ALU.add))
            vop(lambda V: V.reciprocal(t["rd"], t["den"]))
            e_p = vop(lambda V: V.tensor_tensor(
                t["pr"].rearrange("p (r w) -> p r w", w=16),
                t["pr"].rearrange("p (r w) -> p r w", w=16),
                t["rd"].rearrange("p (o w) -> p o w", o=1).broadcast_to([128, 16, 16]),
                op=ALU.mult))
            yield
            em.wait("gpsimd", e_p)
            em.wait("gpsimd", e3d)
            em.wait("vector", e3d)
            em.wait("vector", st["oa_free"])
            oa_d = t["oa"][:, 0, :]
            oa_g = t["oa"][:, 1, :]
            last_d = last_g = None
            e_av_v = e_av_g = None
            for r in range(16):
                pbc = t["pr"][:, 16 * r:16 * (r + 1)].rearrange(
                    "p (w o) -> p w o", o=1).broadcast_to([128, 16, 64])
                vsl = t["vc"][:, 64 * r:64 * r + 1024].rearrange(
                    "p (w d) -> p w d", d=64)
                if r < NAV_DVE:
                    if r == 0:
                        e_av_v = last_d = vop(lambda V, pb=pbc, vs=vsl: V.tensor_tensor(
                            oa_d.rearrange("p (w d) -> p w d", d=64), vs, pb, op=ALU.mult))
                    else:
                        e_av_v = vop(lambda V, pb=pbc, vs=vsl: V.tensor_tensor(
                            t["pd"].rearrange("p (w d) -> p w d", d=64), vs, pb, op=ALU.mult))
                        last_d = vop(lambda V: V.tensor_tensor(oa_d, oa_d, t["pd"], op=ALU.add))
                else:
                    if r == NAV_DVE:
                        e_av_g = last_g = gop(lambda G, pb=pbc, vs=vsl: G.tensor_tensor(
                            oa_g.rearrange("p (w d) -> p w d", d=64), vs, pb, op=ALU.mult))
                    else:
                        e_av_g = gop(lambda G, pb=pbc, vs=vsl: G.tensor_tensor(
                            t["pg"][:, 0, :].rearrange("p (w d) -> p w d", d=64),
                            vs, pb, op=ALU.mult))
                        last_g = gop(lambda G: G.tensor_tensor(
                            oa_g, oa_g, t["pg"][:, 0, :], op=ALU.add))
                yield
            st["v_free"] = (e_av_v, e_av_g)
            em.wait("gpsimd", last_d)
            e_ob = gop(lambda G: G.tensor_tensor(oa_d, oa_d, oa_g, op=ALU.add))
            em.wait("gpsimd", e_ob)
            sp = dma("gpsimd", o_s[seq][(ti % 32) * 128:(ti % 32) * 128 + 128, :],
                     oa_d, f"so{seq}0")
            st["oa_free"] = sp
            spills_out[ti] = sp

        return t, mix_tile_gen, ffn_t

    def make_oproj(mix_base_off, x2_spills, dbg_name):
        em.sboff = mix_base_off
        t = {}
        t["wo"] = em.sb("o_wo", [128, 8, D], BF16)
        t["bo"] = em.sb("o_bo", [128, D], F32)
        t["o2"] = em.sb("o_o2", [128, 2, D], BF16)
        t["x1t"] = em.sb("o_x1t", [128, 2, D], F32)
        t["ofm"] = em.sb("o_ofm", [128, 2, 8, 128], BF16)
        t["x2st"] = em.sb("o_x2st", [128, 2, D], F32)
        t["x2tmp"] = em.sb("o_x2tmp", [128, D], F32)
        assert em.sboff <= CONST_LIMIT

        st = {"init": False, "n_tr": [0], "o2_free": [None, None],
              "x1_done": [None, None], "x2st_free": [None, None], "n_pp": [0]}

        def oproj_tile(ti, mix_done):
            sl = ti % 2
            seq = ti // 32
            al = ti % 32
            head, mh = al // 2, al % 2
            if not st["init"]:
                st["init"] = True
                for eng in ("sync", "vector", "scalar", "tensor", "gpsimd"):
                    for sem in ("sV", "sA", "sP", "sGc"):
                        em.wait(eng, (sem, mix_done[sem]))
                em.wait("sync", cast_w.get("o"))
                if "pre" in phases:
                    em.wait("sync", st_bo)
                dma("sync", t["wo"], wo_s, "sWL3")
                e_bo = dma("sync", t["bo"], bo_s, "sWL3")
                st["wl"] = em.evt("sWL3")
                em.wait("tensor", st["wl"])
            em.wait("sync", em.evt(f"so{seq}0"))
            em.wait("sync", st["o2_free"][sl])
            em.wait("sync", st["x1_done"][sl])
            oq = "oq0" if sl == 0 else "oq1"
            oview = o_s[seq].rearrange("(mh m r) (i d) -> mh m r i d", m=128, r=16, d=64)
            dma("sync", t["o2"][:, sl, :].rearrange("p (r d) -> p r d", d=64),
                oview[mh, :, :, head, :], oq)
            dma("sync", t["x1t"][:, sl], x1_s[ti], oq)
            ld = em.evt(oq)
            dst_fn = lambda c, s2=sl: t["ofm"][:, s2, c, :]
            e_ofm = emit_transposes(t["o2"][:, sl, :], dst_fn, ld, st["n_tr"])
            em.wait("tensor", e_ofm)
            pslot = st["n_pp"][0] % len(PP)
            st["n_pp"][0] += 1
            c0, c1 = PP[pslot]
            xv = psv(c0, c1)
            em.psum_write("tensor", c0, c1)
            for c in range(8):
                for hf in range(2):
                    e_mm = pop(lambda T, c2=c, h2=hf, s2=sl, x2=xv: T.matmul(
                        x2[:, h2 * 512:(h2 + 1) * 512], t["ofm"][:, s2, c2, :],
                        t["wo"][:, c2, h2 * 512:(h2 + 1) * 512],
                        start=(c2 == 0), stop=(c2 == 7)),
                        milestone=(c == 7 and hf == 1))
            em.psum_write_done("tensor", c0, c1, e_mm)
            em.wait("vector", e_mm)
            em.wait("vector", ld)
            em.wait("vector", st["x2st_free"][sl])
            em.psum_read("vector", c0, c1)
            vop(lambda V, s2=sl, x2=xv: V.tensor_tensor(
                t["x2tmp"], x2, t["x1t"][:, s2], op=ALU.add))
            e_fin = vop(lambda V, s2=sl: V.tensor_tensor(
                t["x2st"][:, s2, :], t["x2tmp"], t["bo"], op=ALU.add))
            em.psum_read_done("vector", c0, c1, e_fin)
            em.wait("sync", e_fin)
            ev_sp = dma("sync", x2_s[ti], t["x2st"][:, sl, :], "sxs0" if sl == 0 else "sxs1")
            if dbg_name is not None:
                ev_sp = dma("sync", dbg_t[dbg_name][ti], t["x2st"][:, sl, :],
                            "sxs0" if sl == 0 else "sxs1")
            x2_spills[ti] = ev_sp
            st["x2st_free"][sl] = ev_sp
            st["o2_free"][sl] = em.evt("sP")
            st["x1_done"][sl] = e_fin

        return oproj_tile

    x1_spills = [None] * NTILE
    x2_spills = [None] * NTILE
    ospills = [None] * NTILE
    t1 = alloc_ffn_area("f_")
    ffn_end = em.sboff
    with_mix = "p2" in phases
    if with_mix:
        tm, mix_tile_gen, mix_ffn_ref = make_mix(ffn_end)
        mix_base = ffn_end
        mix_ffn_ref[0] = t1
        gens = deque()

        def pump(k):
            while k > 0 and gens:
                try:
                    next(gens[0])
                    k -= 1
                except StopIteration:
                    gens.popleft()

        def p1_group_start(g):
            if g >= 1:
                batch = em.evt("ss0" if (g - 1) % 2 == 0 else "ss1")
                for i in range(GT):
                    gens.append(mix_tile_gen((g - 1) * GT + i, batch, ospills))

        def p1_evac_hook():
            pump(3)
    else:
        p1_group_start = None
        p1_evac_hook = None
        mix_base = ffn_end
    if "p1" in phases:
        x1_spills = emit_ffn_call(t1, xin, x1_s, True,
                                  "x1" if "x1" in dbg else None,
                                  group_start_hook=p1_group_start,
                                  evac_hook=p1_evac_hook)
        if with_mix:
            batch = em.evt("ss0" if (NGRP - 1) % 2 == 0 else "ss1")
            for i in range(GT):
                gens.append(mix_tile_gen((NGRP - 1) * GT + i, batch, ospills))
            while gens:
                pump(100)
    if "p4" in phases:
        mix_done = {sem: em.cnt[sem] for sem in ("sV", "sA", "sP", "sGc")}
        t4 = alloc_ffn_area("z_")
        if "p3" in phases:
            oproj_tile = make_oproj(mix_base, x2_spills,
                                    "x2" if "x2" in dbg else None)

            def p4_pre_hook(g):
                if g == 0:
                    for i in range(GT):
                        oproj_tile(i, mix_done)

            def p4_expert_hook(g, e):
                if g + 1 < NGRP and e % 2 == 0 and e < 16:
                    oproj_tile((g + 1) * GT + e // 2, mix_done)
        else:
            p4_pre_hook = None
            p4_expert_hook = None
        emit_ffn_call(t4, x2_s, out, False,
                      "x3" if "x3" in dbg else None,
                      pre_group_hook=p4_pre_hook,
                      expert_hook=p4_expert_hook,
                      acc_guard=(em.evt("ss0"), em.evt("ss1")))

    for sname in list(em.cnt):
        if em.cnt[sname]:
            em.emit("sync", lambda E, s=em.sems[sname], v=em.cnt[sname]: E.wait_ge(s, v))

    with nc.Block() as block:
        @block.sync
        def _(E):
            for fn in em.prog["sync"]:
                fn(E)

        @block.gpsimd
        def _(E):
            for fn in em.prog["gpsimd"]:
                fn(E)

        @block.tensor
        def _(E):
            for fn in em.prog["tensor"]:
                fn(E)

        @block.vector
        def _(E):
            for fn in em.prog["vector"]:
                fn(E)

        @block.scalar
        def _(E):
            for fn in em.prog["scalar"]:
                fn(E)

    return nc


def make_in_maps(inputs):
    x = np.asarray(inputs["L0_attn_x"], np.float32)
    shared = dict(
        wqt=np.ascontiguousarray(np.asarray(inputs["L0_attn_Wq"], np.float32).T),
        wkt=np.ascontiguousarray(np.asarray(inputs["L0_attn_Wk"], np.float32).T),
        wvt=np.ascontiguousarray(np.asarray(inputs["L0_attn_Wv"], np.float32).T),
        wot=np.ascontiguousarray(np.asarray(inputs["L0_attn_Wo"], np.float32).T),
        bo=np.ascontiguousarray(np.asarray(inputs["L0_attn_bo"], np.float32).reshape(1, D)),
        ct=np.ascontiguousarray(np.asarray(inputs["L0_ffn_centroids"], np.float32).T),
        eb=np.ascontiguousarray(np.asarray(inputs["L0_ffn_expert_biases"], np.float32).reshape(1, NE)),
        wrt=np.ascontiguousarray(np.asarray(inputs["L0_ffn_Wr"], np.float32).transpose(0, 2, 1)),
        br=np.ascontiguousarray(np.asarray(inputs["L0_ffn_br"], np.float32)),
        wst=np.ascontiguousarray(np.asarray(inputs["L0_ffn_Ws"], np.float32).transpose(0, 2, 1)),
        bs=np.ascontiguousarray(np.asarray(inputs["L0_ffn_bs"], np.float32)),
    )
    maps = []
    for r in range(NCORES):
        m = dict(shared)
        m["xin"] = np.ascontiguousarray(x[2 * r:2 * r + 2].reshape(NTILE, 128, D))
        maps.append(m)
    return maps


def kernel(**inputs):
    in_maps = make_in_maps(inputs)
    nc = build_program()
    res = run_bass_kernel_spmd(nc, in_maps, core_ids=list(range(NCORES)))
    outs = [np.asarray(r["out"], np.float32).reshape(2, SEQ, D) for r in res.results]
    return np.concatenate(outs, axis=0)


# version 03 (final)
# speedup vs baseline: 1.2251x; 1.0369x over previous
from collections import deque

import numpy as np
import concourse.bass as bass
import concourse.mybir as mybir
from concourse.bass_utils import run_bass_kernel_spmd

F32 = mybir.dt.float32
BF16 = mybir.dt.bfloat16
AF = mybir.ActivationFunctionType
ALU = mybir.AluOpType
AX = mybir.AxisListType

NCORES = 8
SEQ = 4096
D = 1024
NH = 16
NE = 16
TOK = 2 * SEQ
NTILE = TOK // 128
GT = 8
NGRP = NTILE // GT
EPS = 1e-5

SBUF_BASE = 17408
SBUF_TOP = 229344

PP = [(0, 1024), (1024, 2048), (2048, 3072)]
TRB = [(3072, 3136), (3584, 3648)]
PSR = [(3200, 3216), (3712, 3728)]
MQ, MK, MV = (0, 1024), (1024, 2048), (2048, 3072)


class Emitter:
    ENGS = ("sync", "gpsimd", "tensor", "vector", "scalar")

    def __init__(self, nc):
        self.nc = nc
        self.prog = {e: [] for e in self.ENGS}
        self.sems = {}
        self.cnt = {}
        self.hw = {e: {} for e in self.ENGS}
        self.sboff = SBUF_BASE
        self.bank = {b: {"w": None, "r": []} for b in range(8)}

    def sb(self, name, shape, dtype):
        per = int(np.prod(shape[1:])) * mybir.dt.size(dtype)
        off = (self.sboff + 31) // 32 * 32
        self.sboff = off + per
        return self.nc.alloc_sbuf_tensor_at(name, shape, dtype, offset=off).ap()

    def sb_at(self, name, shape, dtype, off):
        return self.nc.alloc_sbuf_tensor_at(name, shape, dtype, offset=off).ap()

    def sem(self, name):
        h = self.nc.alloc_semaphore(name=name)
        self.sems[name] = h
        self.cnt[name] = 0
        return h

    def inc(self, name, by=1):
        self.cnt[name] += by
        return (name, self.cnt[name])

    def evt(self, name):
        return (name, self.cnt[name])

    def emit(self, eng, fn):
        self.prog[eng].append(fn)

    def wait(self, eng, evt):
        if evt is None:
            return
        sname, val = evt
        if val <= 0 or self.hw[eng].get(sname, 0) >= val:
            return
        self.hw[eng][sname] = val
        sem = self.sems[sname]
        self.emit(eng, lambda E, s=sem, v=val: E.wait_ge(s, v))

    def _banks(self, c0, c1):
        return range(c0 // 512, (c1 - 1) // 512 + 1)

    def psum_write(self, eng, c0, c1):
        for b in self._banks(c0, c1):
            st = self.bank[b]
            for (s, c, e) in st["r"]:
                if e != eng:
                    self.wait(eng, (s, c))
            if st["w"] is not None and st["w"][2] != eng:
                self.wait(eng, (st["w"][0], st["w"][1]))

    def psum_write_done(self, eng, c0, c1, evt):
        for b in self._banks(c0, c1):
            self.bank[b] = {"w": (evt[0], evt[1], eng), "r": []}

    def psum_read(self, eng, c0, c1):
        for b in self._banks(c0, c1):
            st = self.bank[b]
            if st["w"] is not None and st["w"][2] != eng:
                self.wait(eng, (st["w"][0], st["w"][1]))

    def psum_read_done(self, eng, c0, c1, evt):
        for b in self._banks(c0, c1):
            self.bank[b]["r"].append((evt[0], evt[1], eng))


def build_program(phases=("pre", "p1", "p2", "p3", "p4"), dbg=(), paranoid=False, triv_ln=False):
    nc = bass.Bass("TRN2", target_bir_lowering=False)
    em = Emitter(nc)

    xin = nc.dram_tensor("xin", [NTILE, 128, D], F32, kind="ExternalInput").ap()
    wqt = nc.dram_tensor("wqt", [D, D], F32, kind="ExternalInput").ap()
    wkt = nc.dram_tensor("wkt", [D, D], F32, kind="ExternalInput").ap()
    wvt = nc.dram_tensor("wvt", [D, D], F32, kind="ExternalInput").ap()
    wot = nc.dram_tensor("wot", [D, D], F32, kind="ExternalInput").ap()
    bo_in = nc.dram_tensor("bo", [1, D], F32, kind="ExternalInput").ap()
    ct_in = nc.dram_tensor("ct", [D, NE], F32, kind="ExternalInput").ap()
    eb_in = nc.dram_tensor("eb", [1, NE], F32, kind="ExternalInput").ap()
    wrt_in = nc.dram_tensor("wrt", [NE, D, D], F32, kind="ExternalInput").ap()
    br_in = nc.dram_tensor("br", [NE, D], F32, kind="ExternalInput").ap()
    wst_in = nc.dram_tensor("wst", [2, D, D], F32, kind="ExternalInput").ap()
    bs_in = nc.dram_tensor("bs", [2, D], F32, kind="ExternalInput").ap()
    out = nc.dram_tensor("out", [NTILE, 128, D], F32, kind="ExternalOutput").ap()

    wr_bf = nc.dram_tensor("wr_bf", [NE, 128, 8, D], BF16, kind="Internal").ap()
    wsh_s = nc.dram_tensor("wsh_s", [128, 8, D], BF16, kind="Internal").ap()
    wq_s = nc.dram_tensor("wq_s", [128, 8, D], BF16, kind="Internal").ap()
    wk_s = nc.dram_tensor("wk_s", [128, 8, D], BF16, kind="Internal").ap()
    wv_s = nc.dram_tensor("wv_s", [128, 8, D], BF16, kind="Internal").ap()
    wo_s = nc.dram_tensor("wo_s", [128, 8, D], BF16, kind="Internal").ap()
    x1_s = nc.dram_tensor("x1_s", [NTILE, 128, D], F32, kind="Internal").ap()
    x2_s = nc.dram_tensor("x2_s", [NTILE, 128, D], F32, kind="Internal").ap()
    o_s = nc.dram_tensor("o_s", [2, SEQ, D], BF16, kind="Internal").ap()

    dbg_t = {}
    for name in dbg:
        dbg_t[name] = nc.dram_tensor("dbg_" + name, [NTILE, 128, D], F32,
                                     kind="ExternalOutput").ap()

    ctop = SBUF_TOP

    def const_sb(name, shape, dtype):
        nonlocal ctop
        per = int(np.prod(shape[1:])) * mybir.dt.size(dtype)
        ctop = (ctop - per) // 32 * 32
        return em.sb_at(name, shape, dtype, ctop)

    ident_bf = const_sb("ident_bf", [128, 128], BF16)
    ones_bf = const_sb("ones_bf", [1, 128], BF16)
    ones_f = const_sb("ones_f", [1, 128], F32)
    ct_bf = const_sb("ct_bf", [128, 8, NE], BF16)
    brh_bf = const_sb("brh_bf", [NE, D], BF16)
    bsh4_bf = const_sb("bsh4_bf", [1, D], BF16)
    bo_row = const_sb("bo_row", [1, D], F32)
    bo_bf = const_sb("bo_bf", [1, D], BF16)
    eb_bc = const_sb("eb_bc", [128, NE], F32)
    CONST_LIMIT = ctop

    psa = nc.alloc_psum_tensor("psarena", [128, 4096], F32).ap()

    def psv(c0, c1, bf16=False):
        v = psa[:, c0:c1]
        if bf16:
            v = v.bitcast(BF16)
        return v

    DMA_SEM_NAMES = ("sQ", "sGd", "sCast", "sPre", "sWsh", "sWL2", "sWL3",
                     "sx0", "sx1", "ss0", "ss1", "wb0", "wb1",
                     "so00", "so01", "so10", "so11", "oq0", "oq1", "sxs0", "sxs1", "sx1t0", "sx1t1",
                     "sDbg")
    for s in ("sP", "sV", "sA", "sGc") + DMA_SEM_NAMES:
        em.sem(s)

    def barrier_all(eng):
        if not paranoid:
            return
        for sn in ("sP", "sV", "sA", "sGc", "sGd", "sQ"):
            em.wait(eng, em.evt(sn))

    def vop(make):
        barrier_all("vector")
        evt = em.inc("sV")
        prev = evt[1] - 1
        sem = em.sems["sV"]
        def fn(V, m=make, s=sem, p=prev):
            if p > 0:
                V.wait_ge(s, p)
            m(V).then_inc(s, 1)
        em.emit("vector", fn)
        return evt

    def aop(make):
        barrier_all("scalar")
        evt = em.inc("sA")
        prev = evt[1] - 1
        sem = em.sems["sA"]
        def fn(S, m=make, s=sem, p=prev):
            if p > 0:
                S.wait_ge(s, p)
            m(S).then_inc(s, 1)
        em.emit("scalar", fn)
        return evt

    def gop(make):
        barrier_all("gpsimd")
        evt = em.inc("sGc")
        prev = evt[1] - 1
        sem = em.sems["sGc"]
        def fn(G, m=make, s=sem, p=prev):
            if p > 0:
                G.wait_ge(s, p)
            m(G).then_inc(s, 1)
        em.emit("gpsimd", fn)
        return evt

    def pop(make, milestone=True):
        barrier_all("tensor")
        if milestone:
            evt = em.inc("sP")
            sem = em.sems["sP"]
            em.emit("tensor", lambda T, m=make, s=sem: m(T).then_inc(s, 1))
            return evt
        if paranoid:
            evt = em.inc("sP")
            sem = em.sems["sP"]
            em.emit("tensor", lambda T, m=make, s2=sem: m(T).then_inc(s2, 1))
            return evt
        em.emit("tensor", lambda T, m=make: m(T))
        return None

    def dma(queue, outap, inap, sem_name):
        barrier_all(queue)
        evt = em.inc(sem_name, 16)
        sem = em.sems[sem_name]
        em.emit(queue, lambda E, o=outap, i=inap, s=sem:
                E.dma_start(out=o, in_=i).then_inc(s, 16))
        return evt

    pre_off = em.sboff
    pw0 = em.sb("pw0", [128, 8, D], F32)
    pw1 = em.sb("pw1", [128, 8, D], F32)
    pst = em.sb("pst", [NE, D], F32)
    pbs = em.sb("pbs", [1, 2 * D], F32)
    peb = em.sb("peb", [1, NE], F32)
    psh_bf = em.sb("psh_bf", [128, 8, D], BF16)

    cast_evts = {e: None for e in range(NE)}
    cast_w = {}
    e_brh = e_bsh = e_bobf = None
    st_sh = None
    pre_v_done = None
    pre_p_done = None
    pre_q_done = None

    if "pre" in phases:
        gop(lambda G: G.memset(ident_bf, 0.0))
        gop(lambda G: G.affine_select(out=ident_bf, in_=ident_bf,
                                      compare_op=ALU.not_equal, fill=1.0, base=0,
                                      pattern=[[-1, 128]], channel_multiplier=1))
        gop(lambda G: G.memset(ones_bf, 1.0))
        gop(lambda G: G.memset(ones_f, 1.0))
        e_gc = em.evt("sGc")
        em.wait("tensor", e_gc)
        em.wait("scalar", e_gc)
        em.wait("vector", e_gc)
        cast_w["ct"] = dma("gpsimd", ct_bf, ct_in.rearrange("(c p) e -> p c e", p=128), "sCast")
        for e in range(NE):
            cast_evts[e] = dma("gpsimd", wr_bf[e],
                               wrt_in[e].rearrange("(c p) o -> p c o", p=128), "sCast")
        cast_w["q"] = dma("gpsimd", wq_s, wqt.rearrange("(c p) o -> p c o", p=128), "sCast")
        cast_w["k"] = dma("gpsimd", wk_s, wkt.rearrange("(c p) o -> p c o", p=128), "sCast")
        cast_w["v"] = dma("gpsimd", wv_s, wvt.rearrange("(c p) o -> p c o", p=128), "sCast")
        cast_w["o"] = dma("gpsimd", wo_s, wot.rearrange("(c p) o -> p c o", p=128), "sCast")
        cast_all = em.evt("sCast")
        cast_w = {k: cast_all for k in cast_w}
        cast_evts = {e: cast_all for e in cast_evts}

        ld_w0 = dma("sync", pw0, wst_in[0].rearrange("(c p) o -> p c o", p=128), "sPre")
        ld_w1 = dma("sync", pw1, wst_in[1].rearrange("(c p) o -> p c o", p=128), "sPre")
        ld_br = dma("sync", pst, br_in, "sPre")
        ld_bs = dma("sync", pbs, bs_in.rearrange("a d -> (a d)").unsqueeze(0), "sPre")
        ld_bo = dma("sync", bo_row, bo_in, "sPre")
        ld_eb = dma("sync", peb, eb_in, "sPre")

        pre_batch = em.evt("sPre")
        em.wait("vector", pre_batch)
        vop(lambda V: V.tensor_tensor(pw0.rearrange("p c o -> p (c o)"),
                                      pw0.rearrange("p c o -> p (c o)"),
                                      pw1.rearrange("p c o -> p (c o)"), op=ALU.add))
        e_sh = vop(lambda V: V.tensor_scalar(psh_bf.rearrange("p c o -> p (c o)"),
                                             pw0.rearrange("p c o -> p (c o)"),
                                             0.5, None, op0=ALU.mult))
        em.wait("sync", e_sh)
        st_sh = dma("sync", wsh_s, psh_bf, "sWsh")

        e_brh = vop(lambda V: V.tensor_scalar(brh_bf, pst, 0.5, None, op0=ALU.mult))
        e_bobf = vop(lambda V: V.tensor_copy(bo_bf, bo_row))
        vop(lambda V: V.tensor_tensor(pbs[0:1, 0:D], pbs[0:1, 0:D], pbs[0:1, D:2 * D], op=ALU.add))
        e_bsh = vop(lambda V: V.tensor_scalar(bsh4_bf, pbs[0:1, 0:D], 0.25, None, op0=ALU.mult))

        em.wait("tensor", pre_batch)
        em.psum_write("tensor", 2048, 2064)
        e_mm = pop(lambda T: T.matmul(psv(2048, 2064), ones_f, peb, start=True, stop=True))
        em.psum_write_done("tensor", 2048, 2064, e_mm)
        em.psum_read("vector", 2048, 2064)
        e_ebc = vop(lambda V: V.tensor_copy(eb_bc, psv(2048, 2064)))
        em.psum_read_done("vector", 2048, 2064, e_ebc)

        pre_v_done = em.evt("sV")
        pre_p_done = em.evt("sP")

    def alloc_ffn_area(px):
        em.sboff = pre_off
        t = {}
        t["u_fm"] = em.sb(px + "ufm", [128, 8, GT * 128], BF16)
        t["acc"] = em.sb(px + "acc", [128, GT, D], F32)
        t["wbuf"] = em.sb(px + "wbuf", [128, 2, 8, D], BF16)
        t["uh"] = em.sb(px + "uh", [128, 2, D], BF16)
        t["xt"] = em.sb(px + "xt", [128, 2, D], F32)
        t["xm"] = em.sb(px + "xm", [128, D], F32)
        t["g"] = em.sb(px + "g", [128, GT, NE], F32)
        t["gbf"] = em.sb(px + "gbf", [128, NE], BF16)
        t["gt"] = em.sb(px + "gt", [16, GT, 128], BF16)
        t["rsc"] = em.sb(px + "rsc", [128, 128], F32)
        assert em.sboff <= CONST_LIMIT, f"FFN area overflow {em.sboff}"
        return t

    def emit_ln(t, src_ap, uh_dst, halved, ln_deps, ld_evt, sqd_ap,
                acc_dst=None, acc_deps=()):
        rs = t["rsc"]
        msum, mneg, vsum, svar, stdv, rstd, rstd_h = (
            rs[:, k:k + 1] for k in range(7))
        xm = t["xm"]
        for dep in ln_deps:
            em.wait("vector", dep)
        for dep in acc_deps:
            em.wait("vector", dep)
        em.wait("vector", ld_evt)
        vop(lambda V: V.tensor_reduce(msum, src_ap, axis=AX.X, op=ALU.add))
        vop(lambda V: V.tensor_scalar(mneg, msum, -1.0 / D, None, op0=ALU.mult))
        e_xm = vop(lambda V: V.tensor_scalar(xm, src_ap, mneg, None, op0=ALU.add))
        e_sq = vop(lambda V: V.scalar_tensor_tensor(sqd_ap, xm, 1.0, xm,
                                                    op0=ALU.mult, op1=ALU.mult,
                                                    accum_out=vsum))
        e_vs = vop(lambda V: V.tensor_scalar(svar, vsum, 1.0 / D, EPS,
                                             op0=ALU.mult, op1=ALU.add))
        em.wait("scalar", e_vs)
        aop(lambda S: S.activation(stdv, svar, AF.Ln))
        e_st = aop(lambda S: S.activation(rstd, stdv, AF.Exp, scale=-0.5))
        em.wait("vector", e_st)
        ruse = rstd
        if halved:
            vop(lambda V: V.tensor_scalar(rstd_h, rstd, 0.5, None, op0=ALU.mult))
            ruse = rstd_h
        e_uh = vop(lambda V, r=ruse, ud=uh_dst: V.tensor_scalar(
            ud, xm, r, None, op0=ALU.mult))
        e_acc = None
        if acc_dst is not None:
            e_acc = vop(lambda V, a=acc_dst, s=src_ap: V.scalar_tensor_tensor(
                a, xm, rstd_h, s, op0=ALU.mult, op1=ALU.add))
        return e_uh, e_sq, e_acc

    def emit_transposes(src_full, dst_fn, e_src, n_tr, free_evt=None):
        em.wait("tensor", e_src)
        last = None
        for c in range(8):
            slot = n_tr[0] % 2
            n_tr[0] += 1
            c0, c1 = TRB[slot]
            em.psum_write("tensor", c0, c1)
            trv = psv(c0, c1, bf16=True)
            e_tr = pop(lambda T, c2=c, tv=trv, s=src_full: T.transpose(
                tv, s[:, c2 * 128:(c2 + 1) * 128], ident_bf))
            em.psum_write_done("tensor", c0, c1, e_tr)
            em.psum_read("scalar", c0, c1)
            if c == 0 and free_evt is not None:
                em.wait("scalar", free_evt)
            e_ev = aop(lambda S, d=dst_fn(c), tv=trv: S.copy(d, tv))
            em.psum_read_done("scalar", c0, c1, e_ev)
            last = e_ev
        return last

    def emit_router(t, i, e_ufm, n_r):
        slot = n_r[0] % 2
        n_r[0] += 1
        c0, c1 = PSR[slot]
        prv = psv(c0, c1)
        em.wait("tensor", e_ufm)
        em.wait("tensor", cast_w.get("ct"))
        em.psum_write("tensor", c0, c1)
        for c in range(8):
            e_mm = pop(lambda T, c2=c: T.matmul(
                prv, t["u_fm"][:, c2, i * 128:(i + 1) * 128],
                ct_bf[:, c2, :], start=(c2 == 0), stop=(c2 == 7)),
                milestone=(c == 7))
        em.psum_write_done("tensor", c0, c1, e_mm)
        rs = t["rsc"]
        scf, e_r, s_r = rs[:, 16:32], rs[:, 32:48], rs[:, 48:64]
        msk, mskd = rs[:, 64:80], rs[:, 80:96]
        mn, den, rden = rs[:, 96:97], rs[:, 97:98], rs[:, 98:99]
        m1, m2 = rs[:, 99:100], rs[:, 100:101]
        em.psum_read("vector", c0, c1)
        e_sc = vop(lambda V: V.scalar_tensor_tensor(scf, prv, 2.0, eb_bc,
                                                    op0=ALU.mult, op1=ALU.add))
        em.psum_read_done("vector", c0, c1, e_sc)
        e_mn = vop(lambda V: V.tensor_reduce(mn, scf, axis=AX.X, op=ALU.max, negate=True))
        em.wait("scalar", e_mn)
        e_exp = aop(lambda S: S.activation(e_r, scf, AF.Exp, bias=mn, scale=1.0,
                                           accum_out=den))
        em.wait("vector", e_exp)
        vop(lambda V: V.reciprocal(rden, den))
        vop(lambda V: V.tensor_scalar(s_r, e_r, rden, None, op0=ALU.mult))
        vop(lambda V: V.tensor_reduce(m1, s_r, axis=AX.X, op=ALU.max))
        vop(lambda V: V.tensor_scalar(msk, s_r, m1, None, op0=ALU.is_ge))
        vop(lambda V: V.scalar_tensor_tensor(mskd, msk, -1e9, s_r, op0=ALU.mult, op1=ALU.add))
        vop(lambda V: V.tensor_reduce(m2, mskd, axis=AX.X, op=ALU.max))
        vop(lambda V: V.tensor_scalar(msk, s_r, m2, None, op0=ALU.is_ge))
        e_g = vop(lambda V: V.tensor_tensor(t["g"][:, i, :], s_r, msk, op=ALU.mult))
        e_gb = vop(lambda V: V.tensor_copy(t["gbf"], t["g"][:, i, :]))
        return e_g, e_gb

    def emit_gtr(t, i, e_gb, n_tr):
        em.wait("tensor", e_gb)
        slot = n_tr[0] % 2
        n_tr[0] += 1
        c0, c1 = TRB[slot]
        gv = psv(c0, c1, bf16=True)[0:16, :]
        em.psum_write("tensor", c0, c1)
        e_tr = pop(lambda T: T.transpose(gv, t["gbf"], ident_bf))
        em.psum_write_done("tensor", c0, c1, e_tr)
        em.psum_read("vector", c0, c1)
        e_ev = vop(lambda V: V.tensor_copy(t["gt"][:, i, :], gv))
        em.psum_read_done("vector", c0, c1, e_ev)
        return e_ev

    def emit_ffn_call(t, src_dram, dst_dram, first_call, dbg_name,
                      pre_group_hook=None, post_group_hook=None,
                      expert_hook=None, acc_guard=(),
                      group_start_hook=None, evac_hook=None):
        n_tr, n_r, n_pp = [0], [0], [0]
        ufm_ready = [None]
        g_ready = [None] * GT
        gt_ready = [None] * GT
        uh_free = [None] * GT
        wb_free = [None, None]
        acc_done = [None] * GT
        spills = [None] * NTILE
        spill_batch = [None, None]
        xt_state = {"n": 0, "free": [None, None], "gate": ()}

        def xt_load(ti):
            sl = xt_state["n"] % 2
            xt_state["n"] += 1
            for gv in xt_state["gate"]:
                em.wait("sync", gv)
            em.wait("sync", xt_state["free"][sl])
            ev = dma("sync", t["xt"][:, sl], src_dram[ti], "sx0" if sl == 0 else "sx1")
            return sl, ev

        def stage_a(g):
            last_ufm = None
            if g > 0:
                adeps = [spill_batch[(g - 1) % 2]]
            else:
                adeps = list(acc_guard)
            for i in range(GT):
                ti = g * GT + i
                sl, ev = xt_load(ti)
                usl = i % 2
                deps = [uh_free[usl]] if uh_free[usl] is not None else []
                e_uh, _, e_acc = emit_ln(t, t["xt"][:, sl], t["uh"][:, usl, :],
                                         True, deps, ev, t["acc"][:, i, :],
                                         acc_dst=t["acc"][:, i, :],
                                         acc_deps=adeps)
                xt_state["free"][sl] = e_acc
                dst_fn = lambda c, i2=i: t["u_fm"][:, c, i2 * 128:(i2 + 1) * 128]
                e_ev = emit_transposes(t["uh"][:, usl, :], dst_fn, e_uh, n_tr)
                uh_free[usl] = e_ev
                last_ufm = e_ev
                e_g, e_gb = emit_router(t, i, e_ev, n_r)
                g_ready[i] = e_g
                gt_ready[i] = emit_gtr(t, i, e_gb, n_tr)
            ufm_ready[0] = last_ufm

        def stage_b(g):
            parity = g % 2
            if group_start_hook:
                group_start_hook(g)
            w_evts = []

            def emit_stream(e):
                src = wr_bf[e] if e < 16 else wsh_s
                slot = e % 2
                em.wait("gpsimd", wb_free[slot])
                if "pre" in phases:
                    em.wait("gpsimd", em.evt("sCast"))
                    em.wait("gpsimd", st_sh)
                w_evts.append(dma("gpsimd", t["wbuf"][:, slot], src,
                                  "wb0" if slot == 0 else "wb1"))

            emit_stream(0)
            emit_stream(1)
            for e in range(17):
                slot = e % 2
                em.wait("tensor", w_evts[e])
                em.wait("tensor", ufm_ready[0])
                for i in range(GT):
                    pslot = n_pp[0] % len(PP)
                    n_pp[0] += 1
                    c0, c1 = PP[pslot]
                    ppv = psv(c0, c1)
                    em.psum_write("tensor", c0, c1)
                    ev = None
                    if e == 16:
                        em.wait("tensor", gt_ready[i])
                        em.wait("tensor", e_brh)
                        em.wait("tensor", e_bsh)
                        for h in range(2):
                            pop(lambda T, i2=i, h2=h, pv=ppv: T.matmul(
                                pv[:, h2 * 512:(h2 + 1) * 512],
                                t["gt"][:, i2, :],
                                brh_bf[:, h2 * 512:(h2 + 1) * 512],
                                start=True, stop=False), milestone=False)
                            pop(lambda T, h2=h, pv=ppv: T.matmul(
                                pv[:, h2 * 512:(h2 + 1) * 512], ones_bf,
                                bsh4_bf[:, h2 * 512:(h2 + 1) * 512],
                                start=False, stop=False), milestone=False)
                    for c in range(8):
                        for h in range(2):
                            ml = (c == 7 and h == 1)
                            ev2 = pop(lambda T, i2=i, c2=c, h2=h, pv=ppv, sl2=slot,
                                      e2=e: T.matmul(
                                pv[:, h2 * 512:(h2 + 1) * 512],
                                t["u_fm"][:, c2, i2 * 128:(i2 + 1) * 128],
                                t["wbuf"][:, sl2, c2, h2 * 512:(h2 + 1) * 512],
                                start=(c2 == 0 and e2 != 16), stop=(c2 == 7)),
                                milestone=ml)
                            if ml:
                                ev = ev2
                    em.psum_write_done("tensor", c0, c1, ev)
                    em.wait("vector", ev)
                    em.psum_read("vector", c0, c1)
                    if e == 16:
                        e_evac = vop(lambda V, i2=i, pv=ppv: V.tensor_tensor(
                            t["acc"][:, i2, :], t["acc"][:, i2, :], pv, op=ALU.add))
                    else:
                        em.wait("vector", g_ready[i])
                        gcol = t["g"][:, i, e:e + 1]
                        e_evac = vop(lambda V, i2=i, gc=gcol, pv=ppv: V.scalar_tensor_tensor(
                            t["acc"][:, i2, :], pv, gc, t["acc"][:, i2, :],
                            op0=ALU.mult, op1=ALU.add))
                    em.psum_read_done("vector", c0, c1, e_evac)
                    if e == 16:
                        acc_done[i] = e_evac
                    if evac_hook:
                        evac_hook()
                wb_free[slot] = em.evt("sP")
                if e + 2 <= 16:
                    emit_stream(e + 2)
                if expert_hook:
                    expert_hook(g, e)
            ssem = "ss0" if parity == 0 else "ss1"
            for i in range(GT):
                ti = g * GT + i
                em.wait("sync", acc_done[i])
                spills[ti] = dma("sync", dst_dram[ti], t["acc"][:, i, :], ssem)
                if dbg_name is not None:
                    spills[ti] = dma("sync", dbg_t[dbg_name][ti], t["acc"][:, i, :], ssem)
            spill_batch[parity] = em.evt(ssem)

        if "pre" in phases:
            em.wait("sync", pre_v_done)
            em.wait("sync", pre_p_done)
            em.wait("sync", st_sh)
            em.wait("vector", st_sh)
            em.wait("scalar", st_sh)
            em.wait("gpsimd", pre_v_done)
            em.wait("gpsimd", st_sh)
        if pre_group_hook:
            pre_group_hook(0)
            xt_state["gate"] = (em.evt("sxs0"), em.evt("sxs1"))
        stage_a(0)
        for g in range(NGRP):
            stage_b(g)
            if g + 1 < NGRP:
                if pre_group_hook:
                    pre_group_hook(g + 1)
                    xt_state["gate"] = (em.evt("sxs0"), em.evt("sxs1"))
                stage_a(g + 1)
            if post_group_hook:
                post_group_hook(g)
        return spills

    NROT_DVE = 16
    NAV_DVE = 2

    def make_mix(ffn_end_off):
        em.sboff = ffn_end_off
        t = {}
        t["wq"] = em.sb("m_wq", [128, 8, D], BF16)
        t["wk"] = em.sb("m_wk", [128, 8, D], BF16)
        t["wv"] = em.sb("m_wv", [128, 8, D], BF16)
        t["x1t"] = em.sb("m_x1t", [128, D], F32)
        t["uhm"] = em.sb("m_uhm", [128, D], BF16)
        t["yfm"] = em.sb("m_yfm", [128, 8, 128], BF16)
        t["qc"] = em.sb("m_qc", [128, 1024], BF16)
        t["kc"] = em.sb("m_kc", [128, 2048], BF16)
        t["pdb"] = em.sb("m_pdb", [128, 1024], BF16)
        t["vc"] = em.sb("m_vc", [128, 2048], F32)
        t["ar"] = em.sb("m_ar", [128, 256], F32)
        t["pr"] = em.sb("m_pr", [128, 256], F32)
        t["mx"] = em.sb("m_mx", [128, 16], F32)
        t["den"] = em.sb("m_den", [128, 16], F32)
        t["rd"] = em.sb("m_rd", [128, 16], F32)
        t["pd"] = em.sb("m_pd", [128, 1024], F32)
        t["pg"] = em.sb("m_pg", [128, 2, 1024], F32)
        t["oa"] = em.sb("m_oa", [128, 2, D], F32)
        mix_end = em.sboff
        assert mix_end <= CONST_LIMIT, f"mix overflow {mix_end}"

        st = {"x1_free": None, "uh_free": None,
              "qk_free": (), "v_free": (), "oa_free": None,
              "n_tr": [0], "loaded_w": False, "qkv_free": None}
        ffn_t = [None]

        def mix_tile_gen(ti, x1_ready, spills_out):
            seq = ti // 32
            if not st["loaded_w"]:
                st["loaded_w"] = True
                em.wait("sync", cast_w.get("v"))
                dma("sync", t["wq"], wq_s, "sWL2")
                dma("sync", t["wk"], wk_s, "sWL2")
                dma("sync", t["wv"], wv_s, "sWL2")
                st["wl"] = em.evt("sWL2")
            em.wait("sync", st["x1_free"])
            em.wait("sync", x1_ready)
            ldx = dma("sync", t["x1t"], x1_s[ti], "sx1t0")
            yield
            tshare = dict(t)
            tshare["xm"] = ffn_t[0]["xm"]
            tshare["rsc"] = ffn_t[0]["rsc"]
            deps = [st["uh_free"]] if st["uh_free"] is not None else []
            e_uh, e_sq, _ = emit_ln(tshare, t["x1t"], t["uhm"], False, deps,
                                    ldx, t["x1t"])
            st["x1_free"] = e_sq
            yield
            dst_fn = lambda c: t["yfm"][:, c, :]
            e_yfm = emit_transposes(t["uhm"], dst_fn, e_uh, st["n_tr"],
                                    free_evt=st["qkv_free"])
            st["uh_free"] = e_yfm
            em.wait("tensor", e_yfm)
            em.wait("tensor", st["wl"])
            for rng in (MQ, MK, MV):
                em.psum_write("tensor", rng[0], rng[1])
            e_q = e_k = e_v = None
            for c in range(8):
                lhs = t["yfm"][:, c, :]
                for w, rng in (("wq", MQ), ("wk", MK), ("wv", MV)):
                    for h in range(2):
                        ml = (c == 7 and h == 1)
                        ev = pop(lambda T, l=lhs, w2=w, c2=c, h2=h, r=rng: T.matmul(
                            psv(r[0] + h2 * 512, r[0] + (h2 + 1) * 512), l,
                            t[w2][:, c2, h2 * 512:(h2 + 1) * 512],
                            start=(c2 == 0), stop=(c2 == 7)), milestone=ml)
                        if ml:
                            if w == "wq":
                                e_q = ev
                            elif w == "wk":
                                e_k = ev
                            else:
                                e_v = ev
            st["qkv_free"] = e_v
            em.psum_write_done("tensor", MQ[0], MQ[1], e_q)
            em.psum_write_done("tensor", MK[0], MK[1], e_k)
            em.psum_write_done("tensor", MV[0], MV[1], e_v)
            for evv in st["qk_free"]:
                em.wait("scalar", evv)
            em.wait("scalar", e_q)
            em.psum_read("scalar", MQ[0], MQ[1])
            e1 = aop(lambda S: S.copy(t["qc"], psv(*MQ)))
            em.psum_read_done("scalar", MQ[0], MQ[1], e1)
            em.wait("scalar", e_k)
            em.psum_read("scalar", MK[0], MK[1])
            e2 = aop(lambda S: S.copy(t["kc"][:, 0:1024], psv(*MK)))
            em.psum_read_done("scalar", MK[0], MK[1], e2)
            e2d = aop(lambda S: S.copy(t["kc"][:, 1024:2048], t["kc"][:, 0:1024]))
            for evv in st["v_free"]:
                em.wait("scalar", evv)
            em.wait("scalar", e_v)
            em.psum_read("scalar", MV[0], MV[1])
            e3 = aop(lambda S: S.copy(t["vc"][:, 0:1024], psv(*MV)))
            em.psum_read_done("scalar", MV[0], MV[1], e3)
            e3d = aop(lambda S: S.copy(t["vc"][:, 1024:2048], t["vc"][:, 0:1024]))
            yield
            em.wait("vector", e2d)
            em.wait("gpsimd", e2d)
            att_evts = []
            pg_rd = [None, None]
            e_mv_last = e_mg_last = None
            ng = 0
            for r in range(16):
                ar3 = t["ar"][:, 16 * r:16 * (r + 1)].rearrange("p (i o) -> p i o", o=1)
                if r < NROT_DVE:
                    e_mv_last = vop(lambda V, r2=r: V.tensor_tensor(
                        t["pdb"], t["qc"],
                        t["kc"][:, 64 * r2:64 * r2 + 1024], op=ALU.mult))
                    ev = vop(lambda V, a3=ar3: V.tensor_reduce(
                        a3, t["pdb"].rearrange("p (i d) -> p i d", d=64),
                        axis=AX.X, op=ALU.add))
                else:
                    psl = ng % 2
                    ng += 1
                    em.wait("gpsimd", pg_rd[psl])
                    e_mg_last = gop(lambda G, r2=r, s2=psl: G.tensor_tensor(
                        t["pg"][:, s2, :], t["qc"],
                        t["kc"][:, 64 * r2:64 * r2 + 1024], op=ALU.mult))
                    em.wait("vector", e_mg_last)
                    ev = vop(lambda V, a3=ar3, s2=psl: V.tensor_reduce(
                        a3, t["pg"][:, s2, :].rearrange("p (i d) -> p i d", d=64),
                        axis=AX.X, op=ALU.add))
                    pg_rd[psl] = ev
                att_evts.append(ev)
                yield
            st["qk_free"] = (e_mv_last, e_mg_last)
            for ev in att_evts:
                em.wait("vector", ev)
            arv = t["ar"].rearrange("p (r w) -> p w r", w=16)
            mx3 = t["mx"].rearrange("p (w o) -> p w o", o=1)
            vop(lambda V, a2=arv, m2=mx3: V.tensor_reduce(m2, a2, axis=AX.X, op=ALU.max))
            e_sub = vop(lambda V: V.tensor_tensor(
                t["ar"].rearrange("p (r w) -> p r w", w=16),
                t["ar"].rearrange("p (r w) -> p r w", w=16),
                t["mx"].rearrange("p (o w) -> p o w", o=1).broadcast_to([128, 16, 16]),
                op=ALU.subtract))
            em.wait("scalar", e_sub)
            e_e = aop(lambda S: S.activation(t["pr"], t["ar"], AF.Exp, scale=0.125))
            em.wait("vector", e_e)
            vop(lambda V: V.tensor_reduce(
                t["den"].rearrange("p (w o) -> p w o", o=1),
                t["pr"].rearrange("p (r w) -> p w r", w=16),
                axis=AX.X, op=ALU.add))
            vop(lambda V: V.reciprocal(t["rd"], t["den"]))
            e_p = vop(lambda V: V.tensor_tensor(
                t["pr"].rearrange("p (r w) -> p r w", w=16),
                t["pr"].rearrange("p (r w) -> p r w", w=16),
                t["rd"].rearrange("p (o w) -> p o w", o=1).broadcast_to([128, 16, 16]),
                op=ALU.mult))
            yield
            em.wait("gpsimd", e_p)
            em.wait("gpsimd", e3d)
            em.wait("vector", e3d)
            em.wait("vector", st["oa_free"])
            oa_d = t["oa"][:, 0, :]
            oa_g = t["oa"][:, 1, :]
            last_d = last_g = None
            e_av_v = e_av_g = None
            for r in range(16):
                pbc = t["pr"][:, 16 * r:16 * (r + 1)].rearrange(
                    "p (w o) -> p w o", o=1).broadcast_to([128, 16, 64])
                vsl = t["vc"][:, 64 * r:64 * r + 1024].rearrange(
                    "p (w d) -> p w d", d=64)
                if r < NAV_DVE:
                    if r == 0:
                        e_av_v = last_d = vop(lambda V, pb=pbc, vs=vsl: V.tensor_tensor(
                            oa_d.rearrange("p (w d) -> p w d", d=64), vs, pb, op=ALU.mult))
                    else:
                        e_av_v = vop(lambda V, pb=pbc, vs=vsl: V.tensor_tensor(
                            t["pd"].rearrange("p (w d) -> p w d", d=64), vs, pb, op=ALU.mult))
                        last_d = vop(lambda V: V.tensor_tensor(oa_d, oa_d, t["pd"], op=ALU.add))
                else:
                    if r == NAV_DVE:
                        e_av_g = last_g = gop(lambda G, pb=pbc, vs=vsl: G.tensor_tensor(
                            oa_g.rearrange("p (w d) -> p w d", d=64), vs, pb, op=ALU.mult))
                    else:
                        e_av_g = gop(lambda G, pb=pbc, vs=vsl: G.tensor_tensor(
                            t["pg"][:, 0, :].rearrange("p (w d) -> p w d", d=64),
                            vs, pb, op=ALU.mult))
                        last_g = gop(lambda G: G.tensor_tensor(
                            oa_g, oa_g, t["pg"][:, 0, :], op=ALU.add))
                yield
            st["v_free"] = (e_av_v, e_av_g)
            em.wait("gpsimd", last_d)
            e_ob = gop(lambda G: G.tensor_tensor(oa_d, oa_d, oa_g, op=ALU.add))
            em.wait("gpsimd", e_ob)
            sp = dma("gpsimd", o_s[seq][(ti % 32) * 128:(ti % 32) * 128 + 128, :],
                     oa_d, f"so{seq}0")
            st["oa_free"] = sp
            spills_out[ti] = sp

        return t, mix_tile_gen, ffn_t

    def make_oproj(mix_base_off, x2_spills, dbg_name):
        em.sboff = mix_base_off
        t = {}
        t["wo"] = em.sb("o_wo", [128, 8, D], BF16)
        t["o2"] = em.sb("o_o2", [128, 2, D], BF16)
        t["x1t"] = em.sb("o_x1t", [128, 2, D], F32)
        t["ofm"] = em.sb("o_ofm", [128, 2, 8, 128], BF16)
        t["x2st"] = em.sb("o_x2st", [128, 2, D], F32)
        assert em.sboff <= CONST_LIMIT

        st = {"init": False, "n_tr": [0], "o2_free": [None, None],
              "x1_done": [None, None], "x2st_free": [None, None], "n_pp": [0]}

        def oproj_tile(ti, mix_done):
            sl = ti % 2
            seq = ti // 32
            al = ti % 32
            head, mh = al // 2, al % 2
            if not st["init"]:
                st["init"] = True
                for eng in ("sync", "vector", "scalar", "tensor", "gpsimd"):
                    for sem in ("sV", "sA", "sP", "sGc"):
                        em.wait(eng, (sem, mix_done[sem]))
                em.wait("sync", cast_w.get("o"))
                dma("sync", t["wo"], wo_s, "sWL3")
                st["wl"] = em.evt("sWL3")
                em.wait("tensor", st["wl"])
                if "pre" in phases:
                    em.wait("tensor", e_bobf)
            em.wait("sync", em.evt(f"so{seq}0"))
            em.wait("sync", st["o2_free"][sl])
            em.wait("sync", st["x1_done"][sl])
            oq = "oq0" if sl == 0 else "oq1"
            oview = o_s[seq].rearrange("(mh m r) (i d) -> mh m r i d", m=128, r=16, d=64)
            dma("sync", t["o2"][:, sl, :].rearrange("p (r d) -> p r d", d=64),
                oview[mh, :, :, head, :], oq)
            dma("sync", t["x1t"][:, sl], x1_s[ti], oq)
            ld = em.evt(oq)
            dst_fn = lambda c, s2=sl: t["ofm"][:, s2, c, :]
            e_ofm = emit_transposes(t["o2"][:, sl, :], dst_fn, ld, st["n_tr"])
            em.wait("tensor", e_ofm)
            pslot = st["n_pp"][0] % len(PP)
            st["n_pp"][0] += 1
            c0, c1 = PP[pslot]
            xv = psv(c0, c1)
            em.psum_write("tensor", c0, c1)
            for hf in range(2):
                pop(lambda T, h2=hf, x2=xv: T.matmul(
                    x2[:, h2 * 512:(h2 + 1) * 512], ones_bf,
                    bo_bf[:, h2 * 512:(h2 + 1) * 512],
                    start=True, stop=False), milestone=False)
            for c in range(8):
                for hf in range(2):
                    e_mm = pop(lambda T, c2=c, h2=hf, s2=sl, x2=xv: T.matmul(
                        x2[:, h2 * 512:(h2 + 1) * 512], t["ofm"][:, s2, c2, :],
                        t["wo"][:, c2, h2 * 512:(h2 + 1) * 512],
                        start=False, stop=(c2 == 7)),
                        milestone=(c == 7 and hf == 1))
            em.psum_write_done("tensor", c0, c1, e_mm)
            em.wait("vector", e_mm)
            em.wait("vector", ld)
            em.wait("vector", st["x2st_free"][sl])
            em.psum_read("vector", c0, c1)
            e_fin = vop(lambda V, s2=sl, x2=xv: V.tensor_tensor(
                t["x2st"][:, s2, :], x2, t["x1t"][:, s2], op=ALU.add))
            em.psum_read_done("vector", c0, c1, e_fin)
            em.wait("sync", e_fin)
            ev_sp = dma("sync", x2_s[ti], t["x2st"][:, sl, :], "sxs0" if sl == 0 else "sxs1")
            if dbg_name is not None:
                ev_sp = dma("sync", dbg_t[dbg_name][ti], t["x2st"][:, sl, :],
                            "sxs0" if sl == 0 else "sxs1")
            x2_spills[ti] = ev_sp
            st["x2st_free"][sl] = ev_sp
            st["o2_free"][sl] = em.evt("sP")
            st["x1_done"][sl] = e_fin

        return oproj_tile

    x1_spills = [None] * NTILE
    x2_spills = [None] * NTILE
    ospills = [None] * NTILE
    t1 = alloc_ffn_area("f_")
    ffn_end = em.sboff
    with_mix = "p2" in phases
    if with_mix:
        tm, mix_tile_gen, mix_ffn_ref = make_mix(ffn_end)
        mix_base = ffn_end
        mix_ffn_ref[0] = t1
        gens = deque()

        def pump(k):
            while k > 0 and gens:
                try:
                    next(gens[0])
                    k -= 1
                except StopIteration:
                    gens.popleft()

        def p1_group_start(g):
            if g >= 1:
                batch = em.evt("ss0" if (g - 1) % 2 == 0 else "ss1")
                for i in range(GT):
                    gens.append(mix_tile_gen((g - 1) * GT + i, batch, ospills))

        def p1_evac_hook():
            pump(3)
    else:
        p1_group_start = None
        p1_evac_hook = None
        mix_base = ffn_end
    if "p1" in phases:
        x1_spills = emit_ffn_call(t1, xin, x1_s, True,
                                  "x1" if "x1" in dbg else None,
                                  group_start_hook=p1_group_start,
                                  evac_hook=p1_evac_hook)
        if with_mix:
            batch = em.evt("ss0" if (NGRP - 1) % 2 == 0 else "ss1")
            for i in range(GT):
                gens.append(mix_tile_gen((NGRP - 1) * GT + i, batch, ospills))
            while gens:
                pump(100)
    if "p4" in phases:
        mix_done = {sem: em.cnt[sem] for sem in ("sV", "sA", "sP", "sGc")}
        t4 = alloc_ffn_area("z_")
        if "p3" in phases:
            oproj_tile = make_oproj(mix_base, x2_spills,
                                    "x2" if "x2" in dbg else None)

            def p4_pre_hook(g):
                if g == 0:
                    for i in range(GT):
                        oproj_tile(i, mix_done)

            def p4_expert_hook(g, e):
                if g + 1 < NGRP and e % 2 == 0 and e < 16:
                    oproj_tile((g + 1) * GT + e // 2, mix_done)
        else:
            p4_pre_hook = None
            p4_expert_hook = None
        emit_ffn_call(t4, x2_s, out, False,
                      "x3" if "x3" in dbg else None,
                      pre_group_hook=p4_pre_hook,
                      expert_hook=p4_expert_hook,
                      acc_guard=(em.evt("ss0"), em.evt("ss1")))

    for sname in list(em.cnt):
        if em.cnt[sname]:
            em.emit("sync", lambda E, s=em.sems[sname], v=em.cnt[sname]: E.wait_ge(s, v))

    with nc.Block() as block:
        @block.sync
        def _(E):
            for fn in em.prog["sync"]:
                fn(E)

        @block.gpsimd
        def _(E):
            for fn in em.prog["gpsimd"]:
                fn(E)

        @block.tensor
        def _(E):
            for fn in em.prog["tensor"]:
                fn(E)

        @block.vector
        def _(E):
            for fn in em.prog["vector"]:
                fn(E)

        @block.scalar
        def _(E):
            for fn in em.prog["scalar"]:
                fn(E)

    return nc


def make_in_maps(inputs):
    x = np.asarray(inputs["L0_attn_x"], np.float32)
    shared = dict(
        wqt=np.ascontiguousarray(np.asarray(inputs["L0_attn_Wq"], np.float32).T),
        wkt=np.ascontiguousarray(np.asarray(inputs["L0_attn_Wk"], np.float32).T),
        wvt=np.ascontiguousarray(np.asarray(inputs["L0_attn_Wv"], np.float32).T),
        wot=np.ascontiguousarray(np.asarray(inputs["L0_attn_Wo"], np.float32).T),
        bo=np.ascontiguousarray(np.asarray(inputs["L0_attn_bo"], np.float32).reshape(1, D)),
        ct=np.ascontiguousarray(np.asarray(inputs["L0_ffn_centroids"], np.float32).T),
        eb=np.ascontiguousarray(np.asarray(inputs["L0_ffn_expert_biases"], np.float32).reshape(1, NE)),
        wrt=np.ascontiguousarray(np.asarray(inputs["L0_ffn_Wr"], np.float32).transpose(0, 2, 1)),
        br=np.ascontiguousarray(np.asarray(inputs["L0_ffn_br"], np.float32)),
        wst=np.ascontiguousarray(np.asarray(inputs["L0_ffn_Ws"], np.float32).transpose(0, 2, 1)),
        bs=np.ascontiguousarray(np.asarray(inputs["L0_ffn_bs"], np.float32)),
    )
    maps = []
    for r in range(NCORES):
        m = dict(shared)
        m["xin"] = np.ascontiguousarray(x[2 * r:2 * r + 2].reshape(NTILE, 128, D))
        maps.append(m)
    return maps


def kernel(**inputs):
    in_maps = make_in_maps(inputs)
    nc = build_program()
    res = run_bass_kernel_spmd(nc, in_maps, core_ids=list(range(NCORES)))
    outs = [np.asarray(r["out"], np.float32).reshape(2, SEQ, D) for r in res.results]
    return np.concatenate(outs, axis=0)


# version 04
# speedup vs baseline: 1.2653x; 1.0328x over previous
from collections import deque

import numpy as np
import concourse.bass as bass
import concourse.mybir as mybir
from concourse.bass_utils import run_bass_kernel_spmd

F32 = mybir.dt.float32
BF16 = mybir.dt.bfloat16
AF = mybir.ActivationFunctionType
ALU = mybir.AluOpType
AX = mybir.AxisListType

NCORES = 8
SEQ = 4096
D = 1024
NH = 16
NE = 16
TOK = 2 * SEQ
NTILE = TOK // 128
GT = 8
NGRP = NTILE // GT
EPS = 1e-5

SBUF_BASE = 17408
SBUF_TOP = 229344

PP = [(0, 1024), (1024, 2048), (2048, 3072)]
TRB = [(3072, 3136), (3584, 3648)]
PSR = [(3200, 3216), (3712, 3728)]
MQ, MK, MV = (0, 1024), (1024, 2048), (2048, 3072)


class Emitter:
    ENGS = ("sync", "gpsimd", "tensor", "vector", "scalar")

    def __init__(self, nc):
        self.nc = nc
        self.prog = {e: [] for e in self.ENGS}
        self.sems = {}
        self.cnt = {}
        self.hw = {e: {} for e in self.ENGS}
        self.sboff = SBUF_BASE
        self.bank = {b: {"w": None, "r": []} for b in range(8)}

    def sb(self, name, shape, dtype):
        per = int(np.prod(shape[1:])) * mybir.dt.size(dtype)
        off = (self.sboff + 31) // 32 * 32
        self.sboff = off + per
        return self.nc.alloc_sbuf_tensor_at(name, shape, dtype, offset=off).ap()

    def sb_at(self, name, shape, dtype, off):
        return self.nc.alloc_sbuf_tensor_at(name, shape, dtype, offset=off).ap()

    def sem(self, name):
        h = self.nc.alloc_semaphore(name=name)
        self.sems[name] = h
        self.cnt[name] = 0
        return h

    def inc(self, name, by=1):
        self.cnt[name] += by
        return (name, self.cnt[name])

    def evt(self, name):
        return (name, self.cnt[name])

    def emit(self, eng, fn):
        self.prog[eng].append(fn)

    def wait(self, eng, evt):
        if evt is None:
            return
        sname, val = evt
        if val <= 0 or self.hw[eng].get(sname, 0) >= val:
            return
        self.hw[eng][sname] = val
        sem = self.sems[sname]
        self.emit(eng, lambda E, s=sem, v=val: E.wait_ge(s, v))

    def _banks(self, c0, c1):
        return range(c0 // 512, (c1 - 1) // 512 + 1)

    def psum_write(self, eng, c0, c1):
        for b in self._banks(c0, c1):
            st = self.bank[b]
            for (s, c, e) in st["r"]:
                if e != eng:
                    self.wait(eng, (s, c))
            if st["w"] is not None and st["w"][2] != eng:
                self.wait(eng, (st["w"][0], st["w"][1]))

    def psum_write_done(self, eng, c0, c1, evt):
        for b in self._banks(c0, c1):
            self.bank[b] = {"w": (evt[0], evt[1], eng), "r": []}

    def psum_read(self, eng, c0, c1):
        for b in self._banks(c0, c1):
            st = self.bank[b]
            if st["w"] is not None and st["w"][2] != eng:
                self.wait(eng, (st["w"][0], st["w"][1]))

    def psum_read_done(self, eng, c0, c1, evt):
        for b in self._banks(c0, c1):
            self.bank[b]["r"].append((evt[0], evt[1], eng))


def build_program(phases=("pre", "p1", "p2", "p3", "p4"), dbg=(), paranoid=False, triv_ln=False):
    nc = bass.Bass("TRN2", target_bir_lowering=False)
    em = Emitter(nc)

    xin = nc.dram_tensor("xin", [NTILE, 128, D], F32, kind="ExternalInput").ap()
    wqt = nc.dram_tensor("wqt", [D, D], F32, kind="ExternalInput").ap()
    wkt = nc.dram_tensor("wkt", [D, D], F32, kind="ExternalInput").ap()
    wvt = nc.dram_tensor("wvt", [D, D], F32, kind="ExternalInput").ap()
    wot = nc.dram_tensor("wot", [D, D], F32, kind="ExternalInput").ap()
    bo_in = nc.dram_tensor("bo", [1, D], F32, kind="ExternalInput").ap()
    ct_in = nc.dram_tensor("ct", [D, NE], F32, kind="ExternalInput").ap()
    eb_in = nc.dram_tensor("eb", [1, NE], F32, kind="ExternalInput").ap()
    wrt_in = nc.dram_tensor("wrt", [NE, D, D], F32, kind="ExternalInput").ap()
    br_in = nc.dram_tensor("br", [NE, D], F32, kind="ExternalInput").ap()
    wst_in = nc.dram_tensor("wst", [2, D, D], F32, kind="ExternalInput").ap()
    bs_in = nc.dram_tensor("bs", [2, D], F32, kind="ExternalInput").ap()
    out = nc.dram_tensor("out", [NTILE, 128, D], F32, kind="ExternalOutput").ap()

    wr_bf = nc.dram_tensor("wr_bf", [NE, 128, 8, D], BF16, kind="Internal").ap()
    wsh_s = nc.dram_tensor("wsh_s", [128, 8, D], BF16, kind="Internal").ap()
    wq_s = nc.dram_tensor("wq_s", [128, 8, D], BF16, kind="Internal").ap()
    wk_s = nc.dram_tensor("wk_s", [128, 8, D], BF16, kind="Internal").ap()
    wv_s = nc.dram_tensor("wv_s", [128, 8, D], BF16, kind="Internal").ap()
    wo_s = nc.dram_tensor("wo_s", [128, 8, D], BF16, kind="Internal").ap()
    x1_s = nc.dram_tensor("x1_s", [NTILE, 128, D], F32, kind="Internal").ap()
    x2_s = nc.dram_tensor("x2_s", [NTILE, 128, D], F32, kind="Internal").ap()
    o_s = nc.dram_tensor("o_s", [2, SEQ, D], BF16, kind="Internal").ap()

    dbg_t = {}
    for name in dbg:
        dbg_t[name] = nc.dram_tensor("dbg_" + name, [NTILE, 128, D], F32,
                                     kind="ExternalOutput").ap()

    ctop = SBUF_TOP

    def const_sb(name, shape, dtype):
        nonlocal ctop
        per = int(np.prod(shape[1:])) * mybir.dt.size(dtype)
        ctop = (ctop - per) // 32 * 32
        return em.sb_at(name, shape, dtype, ctop)

    ident_bf = const_sb("ident_bf", [128, 128], BF16)
    ones_bf = const_sb("ones_bf", [1, 128], BF16)
    ones_f = const_sb("ones_f", [1, 128], F32)
    ct_bf = const_sb("ct_bf", [128, 8, NE], BF16)
    brh_bf = const_sb("brh_bf", [NE, D], BF16)
    bsh4_bf = const_sb("bsh4_bf", [1, D], BF16)
    bo_row = const_sb("bo_row", [1, D], F32)
    bo_bf = const_sb("bo_bf", [1, D], BF16)
    eb_bc = const_sb("eb_bc", [128, NE], F32)
    CONST_LIMIT = ctop

    psa = nc.alloc_psum_tensor("psarena", [128, 4096], F32).ap()

    def psv(c0, c1, bf16=False):
        v = psa[:, c0:c1]
        if bf16:
            v = v.bitcast(BF16)
        return v

    DMA_SEM_NAMES = ("sQ", "sGd", "sCast", "sPre", "sWsh", "sWL2", "sWL3",
                     "sx0", "sx1", "ss0", "ss1", "wb0", "wb1",
                     "so00", "so01", "so10", "so11", "oq0", "oq1", "sxs0", "sxs1", "sx1t0", "sx1t1",
                     "sDbg")
    for s in ("sP", "sV", "sA", "sGc") + DMA_SEM_NAMES:
        em.sem(s)

    def barrier_all(eng):
        if not paranoid:
            return
        for sn in ("sP", "sV", "sA", "sGc", "sGd", "sQ"):
            em.wait(eng, em.evt(sn))

    def vop(make):
        barrier_all("vector")
        evt = em.inc("sV")
        prev = evt[1] - 1
        sem = em.sems["sV"]
        def fn(V, m=make, s=sem, p=prev):
            if p > 0:
                V.wait_ge(s, p)
            m(V).then_inc(s, 1)
        em.emit("vector", fn)
        return evt

    def aop(make):
        barrier_all("scalar")
        evt = em.inc("sA")
        prev = evt[1] - 1
        sem = em.sems["sA"]
        def fn(S, m=make, s=sem, p=prev):
            if p > 0:
                S.wait_ge(s, p)
            m(S).then_inc(s, 1)
        em.emit("scalar", fn)
        return evt

    def gop(make):
        barrier_all("gpsimd")
        evt = em.inc("sGc")
        prev = evt[1] - 1
        sem = em.sems["sGc"]
        def fn(G, m=make, s=sem, p=prev):
            if p > 0:
                G.wait_ge(s, p)
            m(G).then_inc(s, 1)
        em.emit("gpsimd", fn)
        return evt

    def pop(make, milestone=True):
        barrier_all("tensor")
        if milestone:
            evt = em.inc("sP")
            sem = em.sems["sP"]
            em.emit("tensor", lambda T, m=make, s=sem: m(T).then_inc(s, 1))
            return evt
        if paranoid:
            evt = em.inc("sP")
            sem = em.sems["sP"]
            em.emit("tensor", lambda T, m=make, s2=sem: m(T).then_inc(s2, 1))
            return evt
        em.emit("tensor", lambda T, m=make: m(T))
        return None

    def dma(queue, outap, inap, sem_name):
        barrier_all(queue)
        evt = em.inc(sem_name, 16)
        sem = em.sems[sem_name]
        em.emit(queue, lambda E, o=outap, i=inap, s=sem:
                E.dma_start(out=o, in_=i).then_inc(s, 16))
        return evt

    pre_off = em.sboff
    pw0 = em.sb("pw0", [128, 8, D], F32)
    pw1 = em.sb("pw1", [128, 8, D], F32)
    pst = em.sb("pst", [NE, D], F32)
    pbs = em.sb("pbs", [1, 2 * D], F32)
    peb = em.sb("peb", [1, NE], F32)
    psh_bf = em.sb("psh_bf", [128, 8, D], BF16)

    cast_evts = {e: None for e in range(NE)}
    cast_w = {}
    e_brh = e_bsh = e_bobf = None
    st_sh = None
    pre_v_done = None
    pre_p_done = None
    pre_q_done = None

    if "pre" in phases:
        gop(lambda G: G.memset(ident_bf, 0.0))
        gop(lambda G: G.affine_select(out=ident_bf, in_=ident_bf,
                                      compare_op=ALU.not_equal, fill=1.0, base=0,
                                      pattern=[[-1, 128]], channel_multiplier=1))
        gop(lambda G: G.memset(ones_bf, 1.0))
        gop(lambda G: G.memset(ones_f, 1.0))
        e_gc = em.evt("sGc")
        em.wait("tensor", e_gc)
        em.wait("scalar", e_gc)
        em.wait("vector", e_gc)
        cast_w["ct"] = dma("gpsimd", ct_bf, ct_in.rearrange("(c p) e -> p c e", p=128), "sCast")
        for e in range(NE):
            cast_evts[e] = dma("gpsimd", wr_bf[e],
                               wrt_in[e].rearrange("(c p) o -> p c o", p=128), "sCast")
        cast_w["q"] = dma("gpsimd", wq_s, wqt.rearrange("(c p) o -> p c o", p=128), "sCast")
        cast_w["k"] = dma("gpsimd", wk_s, wkt.rearrange("(c p) o -> p c o", p=128), "sCast")
        cast_w["v"] = dma("gpsimd", wv_s, wvt.rearrange("(c p) o -> p c o", p=128), "sCast")
        cast_w["o"] = dma("gpsimd", wo_s, wot.rearrange("(c p) o -> p c o", p=128), "sCast")
        cast_all = em.evt("sCast")
        cast_w = {k: cast_all for k in cast_w}
        cast_evts = {e: cast_all for e in cast_evts}

        ld_w0 = dma("sync", pw0, wst_in[0].rearrange("(c p) o -> p c o", p=128), "sPre")
        ld_w1 = dma("sync", pw1, wst_in[1].rearrange("(c p) o -> p c o", p=128), "sPre")
        ld_br = dma("sync", pst, br_in, "sPre")
        ld_bs = dma("sync", pbs, bs_in.rearrange("a d -> (a d)").unsqueeze(0), "sPre")
        ld_bo = dma("sync", bo_row, bo_in, "sPre")
        ld_eb = dma("sync", peb, eb_in, "sPre")

        pre_batch = em.evt("sPre")
        em.wait("vector", pre_batch)
        vop(lambda V: V.tensor_tensor(pw0.rearrange("p c o -> p (c o)"),
                                      pw0.rearrange("p c o -> p (c o)"),
                                      pw1.rearrange("p c o -> p (c o)"), op=ALU.add))
        e_sh = vop(lambda V: V.tensor_scalar(psh_bf.rearrange("p c o -> p (c o)"),
                                             pw0.rearrange("p c o -> p (c o)"),
                                             0.5, None, op0=ALU.mult))
        em.wait("sync", e_sh)
        st_sh = dma("sync", wsh_s, psh_bf, "sWsh")

        e_brh = vop(lambda V: V.tensor_scalar(brh_bf, pst, 0.5, None, op0=ALU.mult))
        e_bobf = vop(lambda V: V.tensor_copy(bo_bf, bo_row))
        vop(lambda V: V.tensor_tensor(pbs[0:1, 0:D], pbs[0:1, 0:D], pbs[0:1, D:2 * D], op=ALU.add))
        e_bsh = vop(lambda V: V.tensor_scalar(bsh4_bf, pbs[0:1, 0:D], 0.25, None, op0=ALU.mult))

        em.wait("tensor", pre_batch)
        em.psum_write("tensor", 2048, 2064)
        e_mm = pop(lambda T: T.matmul(psv(2048, 2064), ones_f, peb, start=True, stop=True))
        em.psum_write_done("tensor", 2048, 2064, e_mm)
        em.psum_read("vector", 2048, 2064)
        e_ebc = vop(lambda V: V.tensor_copy(eb_bc, psv(2048, 2064)))
        em.psum_read_done("vector", 2048, 2064, e_ebc)

        pre_v_done = em.evt("sV")
        pre_p_done = em.evt("sP")

    def alloc_ffn_area(px):
        em.sboff = pre_off
        t = {}
        t["u_fm"] = em.sb(px + "ufm", [128, 8, GT * 128], BF16)
        t["acc"] = em.sb(px + "acc", [128, GT, D], F32)
        t["wbuf"] = em.sb(px + "wbuf", [128, 2, 8, D], BF16)
        t["uh"] = em.sb(px + "uh", [128, 2, D], BF16)
        t["xt"] = em.sb(px + "xt", [128, 2, D], F32)
        t["xm"] = em.sb(px + "xm", [128, D], F32)
        t["g"] = em.sb(px + "g", [128, GT, NE], F32)
        t["gbf"] = em.sb(px + "gbf", [128, NE], BF16)
        t["gt"] = em.sb(px + "gt", [16, GT, 128], BF16)
        t["rsc"] = em.sb(px + "rsc", [128, 128], F32)
        t["xm_guard"] = [None]
        assert em.sboff <= CONST_LIMIT, f"FFN area overflow {em.sboff}"
        return t

    def emit_ln(t, src_ap, uh_dst, halved, ln_deps, ld_evt, sqd_ap,
                acc_dst=None, acc_deps=()):
        rs = t["rsc"]
        msum, mneg, vsum, svar, stdv, rstd, rstd_h = (
            rs[:, k:k + 1] for k in range(7))
        xm = t["xm"]
        for dep in ln_deps:
            em.wait("scalar", dep)
        for dep in acc_deps:
            em.wait("scalar", dep)
        em.wait("scalar", t["xm_guard"][0])
        em.wait("scalar", ld_evt)
        aop(lambda S: S.activation(sqd_ap, src_ap, AF.Copy, accum_out=msum))
        aop(lambda S: S.activation(mneg, msum, AF.Copy, scale=-1.0 / D))
        aop(lambda S: S.activation(xm, src_ap, AF.Identity, bias=mneg))
        e_sq = aop(lambda S: S.activation(sqd_ap, xm, AF.Square, accum_out=vsum))
        aop(lambda S: S.activation(svar, vsum, AF.Copy, scale=1.0 / D, bias=EPS))
        aop(lambda S: S.activation(stdv, svar, AF.Ln))
        aop(lambda S: S.activation(rstd, stdv, AF.Exp, scale=-0.5))
        ruse = rstd
        if halved:
            aop(lambda S: S.activation(rstd_h, rstd, AF.Copy, scale=0.5))
            ruse = rstd_h
        e_uh = aop(lambda S, r=ruse, ud=uh_dst: S.activation(ud, xm, AF.Copy, scale=r))
        e_acc = None
        if acc_dst is not None:
            for dep in acc_deps:
                em.wait("vector", dep)
            em.wait("vector", e_uh)
            e_acc = vop(lambda V, a=acc_dst, s=src_ap: V.scalar_tensor_tensor(
                a, xm, rstd_h, s, op0=ALU.mult, op1=ALU.add))
            t["xm_guard"][0] = e_acc
        else:
            t["xm_guard"][0] = None
        return e_uh, e_sq, e_acc

    def emit_transposes(src_full, dst_fn, e_src, n_tr, free_evt=None):
        em.wait("tensor", e_src)
        last = None
        for c in range(8):
            slot = n_tr[0] % 2
            n_tr[0] += 1
            c0, c1 = TRB[slot]
            em.psum_write("tensor", c0, c1)
            trv = psv(c0, c1, bf16=True)
            e_tr = pop(lambda T, c2=c, tv=trv, s=src_full: T.transpose(
                tv, s[:, c2 * 128:(c2 + 1) * 128], ident_bf))
            em.psum_write_done("tensor", c0, c1, e_tr)
            em.psum_read("scalar", c0, c1)
            if c == 0 and free_evt is not None:
                em.wait("scalar", free_evt)
            e_ev = aop(lambda S, d=dst_fn(c), tv=trv: S.copy(d, tv))
            em.psum_read_done("scalar", c0, c1, e_ev)
            last = e_ev
        return last

    def emit_router(t, i, e_ufm, n_r):
        slot = n_r[0] % 2
        n_r[0] += 1
        c0, c1 = PSR[slot]
        prv = psv(c0, c1)
        em.wait("tensor", e_ufm)
        em.wait("tensor", cast_w.get("ct"))
        em.psum_write("tensor", c0, c1)
        for c in range(8):
            e_mm = pop(lambda T, c2=c: T.matmul(
                prv, t["u_fm"][:, c2, i * 128:(i + 1) * 128],
                ct_bf[:, c2, :], start=(c2 == 0), stop=(c2 == 7)),
                milestone=(c == 7))
        em.psum_write_done("tensor", c0, c1, e_mm)
        rs = t["rsc"]
        scf, e_r, s_r = rs[:, 16:32], rs[:, 32:48], rs[:, 48:64]
        msk, mskd = rs[:, 64:80], rs[:, 80:96]
        mn, den, rden = rs[:, 96:97], rs[:, 97:98], rs[:, 98:99]
        m1, m2 = rs[:, 99:100], rs[:, 100:101]
        em.psum_read("vector", c0, c1)
        e_sc = vop(lambda V: V.scalar_tensor_tensor(scf, prv, 2.0, eb_bc,
                                                    op0=ALU.mult, op1=ALU.add))
        em.psum_read_done("vector", c0, c1, e_sc)
        e_mn = vop(lambda V: V.tensor_reduce(mn, scf, axis=AX.X, op=ALU.max, negate=True))
        em.wait("scalar", e_mn)
        e_exp = aop(lambda S: S.activation(e_r, scf, AF.Exp, bias=mn, scale=1.0,
                                           accum_out=den))
        em.wait("vector", e_exp)
        vop(lambda V: V.reciprocal(rden, den))
        vop(lambda V: V.tensor_scalar(s_r, e_r, rden, None, op0=ALU.mult))
        vop(lambda V: V.tensor_reduce(m1, s_r, axis=AX.X, op=ALU.max))
        vop(lambda V: V.tensor_scalar(msk, s_r, m1, None, op0=ALU.is_ge))
        vop(lambda V: V.scalar_tensor_tensor(mskd, msk, -1e9, s_r, op0=ALU.mult, op1=ALU.add))
        vop(lambda V: V.tensor_reduce(m2, mskd, axis=AX.X, op=ALU.max))
        vop(lambda V: V.tensor_scalar(msk, s_r, m2, None, op0=ALU.is_ge))
        e_g = vop(lambda V: V.tensor_tensor(t["g"][:, i, :], s_r, msk, op=ALU.mult))
        e_gb = vop(lambda V: V.tensor_copy(t["gbf"], t["g"][:, i, :]))
        return e_g, e_gb

    def emit_gtr(t, i, e_gb, n_tr):
        em.wait("tensor", e_gb)
        slot = n_tr[0] % 2
        n_tr[0] += 1
        c0, c1 = TRB[slot]
        gv = psv(c0, c1, bf16=True)[0:16, :]
        em.psum_write("tensor", c0, c1)
        e_tr = pop(lambda T: T.transpose(gv, t["gbf"], ident_bf))
        em.psum_write_done("tensor", c0, c1, e_tr)
        em.psum_read("vector", c0, c1)
        e_ev = vop(lambda V: V.tensor_copy(t["gt"][:, i, :], gv))
        em.psum_read_done("vector", c0, c1, e_ev)
        return e_ev

    def emit_ffn_call(t, src_dram, dst_dram, first_call, dbg_name,
                      pre_group_hook=None, post_group_hook=None,
                      expert_hook=None, acc_guard=(),
                      group_start_hook=None, evac_hook=None):
        n_tr, n_r, n_pp = [0], [0], [0]
        ufm_ready = [None]
        g_ready = [None] * GT
        gt_ready = [None] * GT
        uh_free = [None] * GT
        wb_free = [None, None]
        acc_done = [None] * GT
        spills = [None] * NTILE
        spill_batch = [None, None]
        xt_state = {"n": 0, "free": [None, None], "gate": ()}

        def xt_load(ti):
            sl = xt_state["n"] % 2
            xt_state["n"] += 1
            for gv in xt_state["gate"]:
                em.wait("sync", gv)
            em.wait("sync", xt_state["free"][sl])
            ev = dma("sync", t["xt"][:, sl], src_dram[ti], "sx0" if sl == 0 else "sx1")
            return sl, ev

        def stage_a(g):
            last_ufm = None
            if g > 0:
                adeps = [spill_batch[(g - 1) % 2]]
            else:
                adeps = list(acc_guard)
            for i in range(GT):
                ti = g * GT + i
                sl, ev = xt_load(ti)
                usl = i % 2
                deps = [uh_free[usl]] if uh_free[usl] is not None else []
                e_uh, _, e_acc = emit_ln(t, t["xt"][:, sl], t["uh"][:, usl, :],
                                         True, deps, ev, t["acc"][:, i, :],
                                         acc_dst=t["acc"][:, i, :],
                                         acc_deps=adeps)
                xt_state["free"][sl] = e_acc
                dst_fn = lambda c, i2=i: t["u_fm"][:, c, i2 * 128:(i2 + 1) * 128]
                e_ev = emit_transposes(t["uh"][:, usl, :], dst_fn, e_uh, n_tr)
                uh_free[usl] = e_ev
                last_ufm = e_ev
                e_g, e_gb = emit_router(t, i, e_ev, n_r)
                g_ready[i] = e_g
                gt_ready[i] = emit_gtr(t, i, e_gb, n_tr)
            ufm_ready[0] = last_ufm

        def stage_b(g):
            parity = g % 2
            if group_start_hook:
                group_start_hook(g)
            w_evts = []

            def emit_stream(e):
                src = wr_bf[e] if e < 16 else wsh_s
                slot = e % 2
                em.wait("gpsimd", wb_free[slot])
                if "pre" in phases:
                    em.wait("gpsimd", em.evt("sCast"))
                    em.wait("gpsimd", st_sh)
                w_evts.append(dma("gpsimd", t["wbuf"][:, slot], src,
                                  "wb0" if slot == 0 else "wb1"))

            emit_stream(0)
            emit_stream(1)
            for e in range(17):
                slot = e % 2
                em.wait("tensor", w_evts[e])
                em.wait("tensor", ufm_ready[0])
                for i in range(GT):
                    pslot = n_pp[0] % len(PP)
                    n_pp[0] += 1
                    c0, c1 = PP[pslot]
                    ppv = psv(c0, c1)
                    em.psum_write("tensor", c0, c1)
                    ev = None
                    if e == 16:
                        em.wait("tensor", gt_ready[i])
                        em.wait("tensor", e_brh)
                        em.wait("tensor", e_bsh)
                        for h in range(2):
                            pop(lambda T, i2=i, h2=h, pv=ppv: T.matmul(
                                pv[:, h2 * 512:(h2 + 1) * 512],
                                t["gt"][:, i2, :],
                                brh_bf[:, h2 * 512:(h2 + 1) * 512],
                                start=True, stop=False), milestone=False)
                            pop(lambda T, h2=h, pv=ppv: T.matmul(
                                pv[:, h2 * 512:(h2 + 1) * 512], ones_bf,
                                bsh4_bf[:, h2 * 512:(h2 + 1) * 512],
                                start=False, stop=False), milestone=False)
                    for c in range(8):
                        for h in range(2):
                            ml = (c == 7 and h == 1)
                            ev2 = pop(lambda T, i2=i, c2=c, h2=h, pv=ppv, sl2=slot,
                                      e2=e: T.matmul(
                                pv[:, h2 * 512:(h2 + 1) * 512],
                                t["u_fm"][:, c2, i2 * 128:(i2 + 1) * 128],
                                t["wbuf"][:, sl2, c2, h2 * 512:(h2 + 1) * 512],
                                start=(c2 == 0 and e2 != 16), stop=(c2 == 7)),
                                milestone=ml)
                            if ml:
                                ev = ev2
                    em.psum_write_done("tensor", c0, c1, ev)
                    em.wait("vector", ev)
                    em.psum_read("vector", c0, c1)
                    if e == 16:
                        e_evac = vop(lambda V, i2=i, pv=ppv: V.tensor_tensor(
                            t["acc"][:, i2, :], t["acc"][:, i2, :], pv, op=ALU.add))
                    else:
                        em.wait("vector", g_ready[i])
                        gcol = t["g"][:, i, e:e + 1]
                        e_evac = vop(lambda V, i2=i, gc=gcol, pv=ppv: V.scalar_tensor_tensor(
                            t["acc"][:, i2, :], pv, gc, t["acc"][:, i2, :],
                            op0=ALU.mult, op1=ALU.add))
                    em.psum_read_done("vector", c0, c1, e_evac)
                    if e == 16:
                        acc_done[i] = e_evac
                    if evac_hook:
                        evac_hook()
                wb_free[slot] = em.evt("sP")
                if e + 2 <= 16:
                    emit_stream(e + 2)
                if expert_hook:
                    expert_hook(g, e)
            ssem = "ss0" if parity == 0 else "ss1"
            for i in range(GT):
                ti = g * GT + i
                em.wait("sync", acc_done[i])
                spills[ti] = dma("sync", dst_dram[ti], t["acc"][:, i, :], ssem)
                if dbg_name is not None:
                    spills[ti] = dma("sync", dbg_t[dbg_name][ti], t["acc"][:, i, :], ssem)
            spill_batch[parity] = em.evt(ssem)

        if "pre" in phases:
            em.wait("sync", pre_v_done)
            em.wait("sync", pre_p_done)
            em.wait("sync", st_sh)
            em.wait("vector", st_sh)
            em.wait("scalar", st_sh)
            em.wait("scalar", pre_v_done)
            em.wait("scalar", pre_p_done)
            em.wait("gpsimd", pre_v_done)
            em.wait("gpsimd", st_sh)
        if pre_group_hook:
            pre_group_hook(0)
            xt_state["gate"] = (em.evt("sxs0"), em.evt("sxs1"))
        stage_a(0)
        for g in range(NGRP):
            stage_b(g)
            if g + 1 < NGRP:
                if pre_group_hook:
                    pre_group_hook(g + 1)
                    xt_state["gate"] = (em.evt("sxs0"), em.evt("sxs1"))
                stage_a(g + 1)
            if post_group_hook:
                post_group_hook(g)
        return spills

    NROT_DVE = 16
    NAV_DVE = 2

    def make_mix(ffn_end_off):
        em.sboff = ffn_end_off
        t = {}
        t["wq"] = em.sb("m_wq", [128, 8, D], BF16)
        t["wk"] = em.sb("m_wk", [128, 8, D], BF16)
        t["wv"] = em.sb("m_wv", [128, 8, D], BF16)
        t["x1t"] = em.sb("m_x1t", [128, D], F32)
        t["uhm"] = em.sb("m_uhm", [128, D], BF16)
        t["yfm"] = em.sb("m_yfm", [128, 8, 128], BF16)
        t["qc"] = em.sb("m_qc", [128, 1024], BF16)
        t["kc"] = em.sb("m_kc", [128, 2048], BF16)
        t["pdb"] = em.sb("m_pdb", [128, 1024], BF16)
        t["vc"] = em.sb("m_vc", [128, 2048], F32)
        t["ar"] = em.sb("m_ar", [128, 256], F32)
        t["pr"] = em.sb("m_pr", [128, 256], F32)
        t["mx"] = em.sb("m_mx", [128, 16], F32)
        t["den"] = em.sb("m_den", [128, 16], F32)
        t["rd"] = em.sb("m_rd", [128, 16], F32)
        t["pd"] = em.sb("m_pd", [128, 1024], F32)
        t["pg"] = em.sb("m_pg", [128, 2, 1024], F32)
        t["oa"] = em.sb("m_oa", [128, 2, D], F32)
        mix_end = em.sboff
        assert mix_end <= CONST_LIMIT, f"mix overflow {mix_end}"

        st = {"x1_free": None, "uh_free": None,
              "qk_free": (), "v_free": (), "oa_free": None,
              "n_tr": [0], "loaded_w": False, "qkv_free": None}
        ffn_t = [None]

        def mix_tile_gen(ti, x1_ready, spills_out):
            seq = ti // 32
            if not st["loaded_w"]:
                st["loaded_w"] = True
                em.wait("sync", cast_w.get("v"))
                dma("sync", t["wq"], wq_s, "sWL2")
                dma("sync", t["wk"], wk_s, "sWL2")
                dma("sync", t["wv"], wv_s, "sWL2")
                st["wl"] = em.evt("sWL2")
            em.wait("sync", st["x1_free"])
            em.wait("sync", x1_ready)
            ldx = dma("sync", t["x1t"], x1_s[ti], "sx1t0")
            yield
            tshare = dict(t)
            tshare["xm"] = ffn_t[0]["xm"]
            tshare["rsc"] = ffn_t[0]["rsc"]
            tshare["xm_guard"] = ffn_t[0]["xm_guard"]
            deps = [st["uh_free"]] if st["uh_free"] is not None else []
            e_uh, e_sq, _ = emit_ln(tshare, t["x1t"], t["uhm"], False, deps,
                                    ldx, t["x1t"])
            st["x1_free"] = e_sq
            yield
            dst_fn = lambda c: t["yfm"][:, c, :]
            e_yfm = emit_transposes(t["uhm"], dst_fn, e_uh, st["n_tr"],
                                    free_evt=st["qkv_free"])
            st["uh_free"] = e_yfm
            em.wait("tensor", e_yfm)
            em.wait("tensor", st["wl"])
            for rng in (MQ, MK, MV):
                em.psum_write("tensor", rng[0], rng[1])
            e_q = e_k = e_v = None
            for c in range(8):
                lhs = t["yfm"][:, c, :]
                for w, rng in (("wq", MQ), ("wk", MK), ("wv", MV)):
                    for h in range(2):
                        ml = (c == 7 and h == 1)
                        ev = pop(lambda T, l=lhs, w2=w, c2=c, h2=h, r=rng: T.matmul(
                            psv(r[0] + h2 * 512, r[0] + (h2 + 1) * 512), l,
                            t[w2][:, c2, h2 * 512:(h2 + 1) * 512],
                            start=(c2 == 0), stop=(c2 == 7)), milestone=ml)
                        if ml:
                            if w == "wq":
                                e_q = ev
                            elif w == "wk":
                                e_k = ev
                            else:
                                e_v = ev
            st["qkv_free"] = e_v
            em.psum_write_done("tensor", MQ[0], MQ[1], e_q)
            em.psum_write_done("tensor", MK[0], MK[1], e_k)
            em.psum_write_done("tensor", MV[0], MV[1], e_v)
            for evv in st["qk_free"]:
                em.wait("scalar", evv)
            em.wait("scalar", e_q)
            em.psum_read("scalar", MQ[0], MQ[1])
            e1 = aop(lambda S: S.copy(t["qc"], psv(*MQ)))
            em.psum_read_done("scalar", MQ[0], MQ[1], e1)
            em.wait("scalar", e_k)
            em.psum_read("scalar", MK[0], MK[1])
            e2 = aop(lambda S: S.copy(t["kc"][:, 0:1024], psv(*MK)))
            em.psum_read_done("scalar", MK[0], MK[1], e2)
            e2d = aop(lambda S: S.copy(t["kc"][:, 1024:2048], t["kc"][:, 0:1024]))
            for evv in st["v_free"]:
                em.wait("scalar", evv)
            em.wait("scalar", e_v)
            em.psum_read("scalar", MV[0], MV[1])
            e3 = aop(lambda S: S.copy(t["vc"][:, 0:1024], psv(*MV)))
            em.psum_read_done("scalar", MV[0], MV[1], e3)
            e3d = aop(lambda S: S.copy(t["vc"][:, 1024:2048], t["vc"][:, 0:1024]))
            yield
            em.wait("vector", e2d)
            em.wait("gpsimd", e2d)
            att_evts = []
            pg_rd = [None, None]
            e_mv_last = e_mg_last = None
            ng = 0
            for r in range(16):
                ar3 = t["ar"][:, 16 * r:16 * (r + 1)].rearrange("p (i o) -> p i o", o=1)
                if r < NROT_DVE:
                    e_mv_last = vop(lambda V, r2=r: V.tensor_tensor(
                        t["pdb"], t["qc"],
                        t["kc"][:, 64 * r2:64 * r2 + 1024], op=ALU.mult))
                    ev = vop(lambda V, a3=ar3: V.tensor_reduce(
                        a3, t["pdb"].rearrange("p (i d) -> p i d", d=64),
                        axis=AX.X, op=ALU.add))
                else:
                    psl = ng % 2
                    ng += 1
                    em.wait("gpsimd", pg_rd[psl])
                    e_mg_last = gop(lambda G, r2=r, s2=psl: G.tensor_tensor(
                        t["pg"][:, s2, :], t["qc"],
                        t["kc"][:, 64 * r2:64 * r2 + 1024], op=ALU.mult))
                    em.wait("vector", e_mg_last)
                    ev = vop(lambda V, a3=ar3, s2=psl: V.tensor_reduce(
                        a3, t["pg"][:, s2, :].rearrange("p (i d) -> p i d", d=64),
                        axis=AX.X, op=ALU.add))
                    pg_rd[psl] = ev
                att_evts.append(ev)
                yield
            st["qk_free"] = (e_mv_last, e_mg_last)
            for ev in att_evts:
                em.wait("vector", ev)
            arv = t["ar"].rearrange("p (r w) -> p w r", w=16)
            mx3 = t["mx"].rearrange("p (w o) -> p w o", o=1)
            vop(lambda V, a2=arv, m2=mx3: V.tensor_reduce(m2, a2, axis=AX.X, op=ALU.max))
            e_sub = vop(lambda V: V.tensor_tensor(
                t["ar"].rearrange("p (r w) -> p r w", w=16),
                t["ar"].rearrange("p (r w) -> p r w", w=16),
                t["mx"].rearrange("p (o w) -> p o w", o=1).broadcast_to([128, 16, 16]),
                op=ALU.subtract))
            em.wait("scalar", e_sub)
            e_e = aop(lambda S: S.activation(t["pr"], t["ar"], AF.Exp, scale=0.125))
            em.wait("vector", e_e)
            vop(lambda V: V.tensor_reduce(
                t["den"].rearrange("p (w o) -> p w o", o=1),
                t["pr"].rearrange("p (r w) -> p w r", w=16),
                axis=AX.X, op=ALU.add))
            vop(lambda V: V.reciprocal(t["rd"], t["den"]))
            e_p = vop(lambda V: V.tensor_tensor(
                t["pr"].rearrange("p (r w) -> p r w", w=16),
                t["pr"].rearrange("p (r w) -> p r w", w=16),
                t["rd"].rearrange("p (o w) -> p o w", o=1).broadcast_to([128, 16, 16]),
                op=ALU.mult))
            yield
            em.wait("gpsimd", e_p)
            em.wait("gpsimd", e3d)
            em.wait("vector", e3d)
            em.wait("vector", st["oa_free"])
            oa_d = t["oa"][:, 0, :]
            oa_g = t["oa"][:, 1, :]
            last_d = last_g = None
            e_av_v = e_av_g = None
            for r in range(16):
                pbc = t["pr"][:, 16 * r:16 * (r + 1)].rearrange(
                    "p (w o) -> p w o", o=1).broadcast_to([128, 16, 64])
                vsl = t["vc"][:, 64 * r:64 * r + 1024].rearrange(
                    "p (w d) -> p w d", d=64)
                if r < NAV_DVE:
                    if r == 0:
                        e_av_v = last_d = vop(lambda V, pb=pbc, vs=vsl: V.tensor_tensor(
                            oa_d.rearrange("p (w d) -> p w d", d=64), vs, pb, op=ALU.mult))
                    else:
                        e_av_v = vop(lambda V, pb=pbc, vs=vsl: V.tensor_tensor(
                            t["pd"].rearrange("p (w d) -> p w d", d=64), vs, pb, op=ALU.mult))
                        last_d = vop(lambda V: V.tensor_tensor(oa_d, oa_d, t["pd"], op=ALU.add))
                else:
                    if r == NAV_DVE:
                        e_av_g = last_g = gop(lambda G, pb=pbc, vs=vsl: G.tensor_tensor(
                            oa_g.rearrange("p (w d) -> p w d", d=64), vs, pb, op=ALU.mult))
                    else:
                        e_av_g = gop(lambda G, pb=pbc, vs=vsl: G.tensor_tensor(
                            t["pg"][:, 0, :].rearrange("p (w d) -> p w d", d=64),
                            vs, pb, op=ALU.mult))
                        last_g = gop(lambda G: G.tensor_tensor(
                            oa_g, oa_g, t["pg"][:, 0, :], op=ALU.add))
                yield
            st["v_free"] = (e_av_v, e_av_g)
            em.wait("gpsimd", last_d)
            e_ob = gop(lambda G: G.tensor_tensor(oa_d, oa_d, oa_g, op=ALU.add))
            em.wait("gpsimd", e_ob)
            sp = dma("gpsimd", o_s[seq][(ti % 32) * 128:(ti % 32) * 128 + 128, :],
                     oa_d, f"so{seq}0")
            st["oa_free"] = sp
            spills_out[ti] = sp

        return t, mix_tile_gen, ffn_t

    def make_oproj(mix_base_off, x2_spills, dbg_name):
        em.sboff = mix_base_off
        t = {}
        t["wo"] = em.sb("o_wo", [128, 8, D], BF16)
        t["o2"] = em.sb("o_o2", [128, 2, D], BF16)
        t["x1t"] = em.sb("o_x1t", [128, 2, D], F32)
        t["ofm"] = em.sb("o_ofm", [128, 2, 8, 128], BF16)
        t["x2st"] = em.sb("o_x2st", [128, 2, D], F32)
        assert em.sboff <= CONST_LIMIT

        st = {"init": False, "n_tr": [0], "o2_free": [None, None],
              "x1_done": [None, None], "x2st_free": [None, None], "n_pp": [0]}

        def oproj_tile(ti, mix_done):
            sl = ti % 2
            seq = ti // 32
            al = ti % 32
            head, mh = al // 2, al % 2
            if not st["init"]:
                st["init"] = True
                for eng in ("sync", "vector", "scalar", "tensor", "gpsimd"):
                    for sem in ("sV", "sA", "sP", "sGc"):
                        em.wait(eng, (sem, mix_done[sem]))
                em.wait("sync", cast_w.get("o"))
                dma("sync", t["wo"], wo_s, "sWL3")
                st["wl"] = em.evt("sWL3")
                em.wait("tensor", st["wl"])
                if "pre" in phases:
                    em.wait("tensor", e_bobf)
            em.wait("sync", em.evt(f"so{seq}0"))
            em.wait("sync", st["o2_free"][sl])
            em.wait("sync", st["x1_done"][sl])
            oq = "oq0" if sl == 0 else "oq1"
            oview = o_s[seq].rearrange("(mh m r) (i d) -> mh m r i d", m=128, r=16, d=64)
            dma("sync", t["o2"][:, sl, :].rearrange("p (r d) -> p r d", d=64),
                oview[mh, :, :, head, :], oq)
            dma("sync", t["x1t"][:, sl], x1_s[ti], oq)
            ld = em.evt(oq)
            dst_fn = lambda c, s2=sl: t["ofm"][:, s2, c, :]
            e_ofm = emit_transposes(t["o2"][:, sl, :], dst_fn, ld, st["n_tr"])
            em.wait("tensor", e_ofm)
            pslot = st["n_pp"][0] % len(PP)
            st["n_pp"][0] += 1
            c0, c1 = PP[pslot]
            xv = psv(c0, c1)
            em.psum_write("tensor", c0, c1)
            for hf in range(2):
                pop(lambda T, h2=hf, x2=xv: T.matmul(
                    x2[:, h2 * 512:(h2 + 1) * 512], ones_bf,
                    bo_bf[:, h2 * 512:(h2 + 1) * 512],
                    start=True, stop=False), milestone=False)
            for c in range(8):
                for hf in range(2):
                    e_mm = pop(lambda T, c2=c, h2=hf, s2=sl, x2=xv: T.matmul(
                        x2[:, h2 * 512:(h2 + 1) * 512], t["ofm"][:, s2, c2, :],
                        t["wo"][:, c2, h2 * 512:(h2 + 1) * 512],
                        start=False, stop=(c2 == 7)),
                        milestone=(c == 7 and hf == 1))
            em.psum_write_done("tensor", c0, c1, e_mm)
            em.wait("vector", e_mm)
            em.wait("vector", ld)
            em.wait("vector", st["x2st_free"][sl])
            em.psum_read("vector", c0, c1)
            e_fin = vop(lambda V, s2=sl, x2=xv: V.tensor_tensor(
                t["x2st"][:, s2, :], x2, t["x1t"][:, s2], op=ALU.add))
            em.psum_read_done("vector", c0, c1, e_fin)
            em.wait("sync", e_fin)
            ev_sp = dma("sync", x2_s[ti], t["x2st"][:, sl, :], "sxs0" if sl == 0 else "sxs1")
            if dbg_name is not None:
                ev_sp = dma("sync", dbg_t[dbg_name][ti], t["x2st"][:, sl, :],
                            "sxs0" if sl == 0 else "sxs1")
            x2_spills[ti] = ev_sp
            st["x2st_free"][sl] = ev_sp
            st["o2_free"][sl] = em.evt("sP")
            st["x1_done"][sl] = e_fin

        return oproj_tile

    x1_spills = [None] * NTILE
    x2_spills = [None] * NTILE
    ospills = [None] * NTILE
    t1 = alloc_ffn_area("f_")
    ffn_end = em.sboff
    with_mix = "p2" in phases
    if with_mix:
        tm, mix_tile_gen, mix_ffn_ref = make_mix(ffn_end)
        mix_base = ffn_end
        mix_ffn_ref[0] = t1
        gens = deque()

        def pump(k):
            while k > 0 and gens:
                try:
                    next(gens[0])
                    k -= 1
                except StopIteration:
                    gens.popleft()

        def p1_group_start(g):
            if g >= 1:
                batch = em.evt("ss0" if (g - 1) % 2 == 0 else "ss1")
                for i in range(GT):
                    gens.append(mix_tile_gen((g - 1) * GT + i, batch, ospills))

        def p1_evac_hook():
            pump(3)
    else:
        p1_group_start = None
        p1_evac_hook = None
        mix_base = ffn_end
    if "p1" in phases:
        x1_spills = emit_ffn_call(t1, xin, x1_s, True,
                                  "x1" if "x1" in dbg else None,
                                  group_start_hook=p1_group_start,
                                  evac_hook=p1_evac_hook)
        if with_mix:
            batch = em.evt("ss0" if (NGRP - 1) % 2 == 0 else "ss1")
            for i in range(GT):
                gens.append(mix_tile_gen((NGRP - 1) * GT + i, batch, ospills))
            while gens:
                pump(100)
    if "p4" in phases:
        mix_done = {sem: em.cnt[sem] for sem in ("sV", "sA", "sP", "sGc")}
        t4 = alloc_ffn_area("z_")
        if "p3" in phases:
            oproj_tile = make_oproj(mix_base, x2_spills,
                                    "x2" if "x2" in dbg else None)

            def p4_pre_hook(g):
                if g == 0:
                    for i in range(GT):
                        oproj_tile(i, mix_done)

            def p4_expert_hook(g, e):
                if g + 1 < NGRP and e % 2 == 0 and e < 16:
                    oproj_tile((g + 1) * GT + e // 2, mix_done)
        else:
            p4_pre_hook = None
            p4_expert_hook = None
        emit_ffn_call(t4, x2_s, out, False,
                      "x3" if "x3" in dbg else None,
                      pre_group_hook=p4_pre_hook,
                      expert_hook=p4_expert_hook,
                      acc_guard=(em.evt("ss0"), em.evt("ss1")))

    for sname in list(em.cnt):
        if em.cnt[sname]:
            em.emit("sync", lambda E, s=em.sems[sname], v=em.cnt[sname]: E.wait_ge(s, v))

    with nc.Block() as block:
        @block.sync
        def _(E):
            for fn in em.prog["sync"]:
                fn(E)

        @block.gpsimd
        def _(E):
            for fn in em.prog["gpsimd"]:
                fn(E)

        @block.tensor
        def _(E):
            for fn in em.prog["tensor"]:
                fn(E)

        @block.vector
        def _(E):
            for fn in em.prog["vector"]:
                fn(E)

        @block.scalar
        def _(E):
            for fn in em.prog["scalar"]:
                fn(E)

    return nc


def make_in_maps(inputs):
    x = np.asarray(inputs["L0_attn_x"], np.float32)
    shared = dict(
        wqt=np.ascontiguousarray(np.asarray(inputs["L0_attn_Wq"], np.float32).T),
        wkt=np.ascontiguousarray(np.asarray(inputs["L0_attn_Wk"], np.float32).T),
        wvt=np.ascontiguousarray(np.asarray(inputs["L0_attn_Wv"], np.float32).T),
        wot=np.ascontiguousarray(np.asarray(inputs["L0_attn_Wo"], np.float32).T),
        bo=np.ascontiguousarray(np.asarray(inputs["L0_attn_bo"], np.float32).reshape(1, D)),
        ct=np.ascontiguousarray(np.asarray(inputs["L0_ffn_centroids"], np.float32).T),
        eb=np.ascontiguousarray(np.asarray(inputs["L0_ffn_expert_biases"], np.float32).reshape(1, NE)),
        wrt=np.ascontiguousarray(np.asarray(inputs["L0_ffn_Wr"], np.float32).transpose(0, 2, 1)),
        br=np.ascontiguousarray(np.asarray(inputs["L0_ffn_br"], np.float32)),
        wst=np.ascontiguousarray(np.asarray(inputs["L0_ffn_Ws"], np.float32).transpose(0, 2, 1)),
        bs=np.ascontiguousarray(np.asarray(inputs["L0_ffn_bs"], np.float32)),
    )
    maps = []
    for r in range(NCORES):
        m = dict(shared)
        m["xin"] = np.ascontiguousarray(x[2 * r:2 * r + 2].reshape(NTILE, 128, D))
        maps.append(m)
    return maps


def kernel(**inputs):
    in_maps = make_in_maps(inputs)
    nc = build_program()
    res = run_bass_kernel_spmd(nc, in_maps, core_ids=list(range(NCORES)))
    outs = [np.asarray(r["out"], np.float32).reshape(2, SEQ, D) for r in res.results]
    return np.concatenate(outs, axis=0)
